# version 09
# baseline (speedup 1.0000x reference)
import math
import os
from types import SimpleNamespace

import numpy as np
import ml_dtypes

import concourse.bass as bass
import concourse.mybir as mybir
from concourse.bass_utils import run_bass_kernel_spmd
from contextlib import ExitStack

F32 = mybir.dt.float32
BF16 = mybir.dt.bfloat16
I8 = mybir.dt.int8
AF = mybir.ActivationFunctionType
ALU = mybir.AluOpType
AXX = mybir.AxisListType.X

B, S, H = 2, 4096, 2048
NQ, NKV, D, RANK = 16, 8, 128, 8
INTER = 8192
NCH = H // 128
SLOTS = 8
T = SLOTS * 128
KB = S // 128
TC = 8
NIB = INTER // 128
SCALE = 1.0 / math.sqrt(D)
EPS = 1e-5

PAGE = 2048
BF2 = np.dtype(ml_dtypes.bfloat16)
SAFE_SYNC = os.environ.get("BASS_SIM_FP64") == "1"


class Sched:
    ENGS = ("pe", "dv", "ac", "sy", "gp")

    def __init__(self):
        self.ops = []
        self.lastw = {}
        self.readers = {}

    def add(self, eng, fn, reads=(), writes=(), dma=False, deps=(), stream=None, tag=""):
        op = SimpleNamespace(i=len(self.ops), eng=eng, fn=fn, dma=dma,
                             deps=set(), need_inc=False, cum=0, clock=None,
                             semkey=None, waits=None, stream=stream, tag=tag)
        for k in reads:
            w = self.lastw.get(k)
            if w is not None:
                op.deps.add(w.i)
            self.readers.setdefault(k, []).append(op)
        for k in writes:
            w = self.lastw.get(k)
            if w is not None:
                op.deps.add(w.i)
            for rop in self.readers.get(k, ()):
                op.deps.add(rop.i)
            self.lastw[k] = op
            self.readers[k] = []
        for d in deps:
            op.deps.add(d.i)
        op.deps.discard(op.i)
        self.ops.append(op)
        return op

    def finalize(self):
        for op in self.ops:
            if op.dma:
                op.need_inc = True
        for op in self.ops:
            for d in op.deps:
                dop = self.ops[d]
                if dop.dma or dop.eng != op.eng or SAFE_SYNC:
                    dop.need_inc = True
        cnt = {}
        for op in self.ops:
            op.semkey = ("D:" + (op.stream or op.eng)) if op.dma else op.eng
            if op.need_inc:
                cnt[op.semkey] = cnt.get(op.semkey, 0) + (16 if op.dma else 1)
                op.cum = cnt[op.semkey]
        for op in self.ops:
            cl = {}
            for d in op.deps:
                for k, v in self.ops[d].clock.items():
                    if cl.get(k, 0) < v:
                        cl[k] = v
            if op.need_inc and op.cum:
                cl[op.semkey] = op.cum
            op.clock = cl
        eclock = {e: {} for e in self.ENGS}
        self.per_eng = {e: [] for e in self.ENGS}
        for op in self.ops:
            ec = eclock[op.eng]
            waits = []
            for d in sorted(op.deps):
                dop = self.ops[d]
                if dop.eng == op.eng and not dop.dma and not SAFE_SYNC:
                    continue
                if dop.cum and ec.get(dop.semkey, 0) < dop.cum:
                    waits.append((dop.semkey, dop.cum))
                    for k, v in dop.clock.items():
                        if ec.get(k, 0) < v:
                            ec[k] = v
            op.waits = waits
            self.per_eng[op.eng].append(op)
        return cnt

    def emit(self, eng_name, engobj, sems):
        for op in self.per_eng[eng_name]:
            for sk, val in op.waits:
                engobj.wait_ge(sems[sk], val)
            inst = op.fn(engobj)
            if op.need_inc:
                assert inst is not None, f"op {op.i} on {op.eng} needs inc but returned no instruction"
                inst.then_inc(sems[op.semkey], 16 if op.dma else 1)


class Arena:

    def __init__(self, nc, nbytes):
        self.nc = nc
        self.nbytes = nbytes
        self.handle = nc.alloc_sbuf_tensor("arena", [128, nbytes], mybir.dt.uint8)
        self.base = nc.lookup_mloc(self.handle).addr
        self.tensors = {}

    def tensor(self, name, off, shape, dtype):
        size = int(np.prod(shape[1:])) * mybir.dt.size(dtype)
        assert off % 32 == 0 and off + size <= self.nbytes, (name, off, size, self.nbytes)
        h = self.nc.virtual_sbuf_tensor(name, list(shape), dtype)
        self.nc._tensor(h.name, list(shape), dtype, type="SB")
        mloc = self.nc.lookup_mloc(h)
        mloc.allocated = not self.nc.target_bir_lowering
        mloc.addr = self.base + off
        self.tensors[name] = (h, off, size)
        return h

    def keys(self, name, el_lo=None, el_hi=None):
        h, off, size = self.tensors[name]
        es = mybir.dt.size(h.dtype)
        lo = off + (0 if el_lo is None else el_lo * es)
        hi = off + (size if el_hi is None else el_hi * es)
        return [("S", p) for p in range(lo // PAGE, (hi + PAGE - 1) // PAGE)]


def _perm_head():
    return np.concatenate([np.arange(0, D, 2), np.arange(1, D, 2)])


def _rope_tables(positions):
    j = np.arange(0, D, 2, dtype=np.float32)
    inv = (1.0 / (10000.0 ** (j / np.float32(D)))).astype(np.float32)
    t = positions.astype(np.float32)
    freqs = (inv[:, None] * t[None, :]).astype(np.float32)
    c = np.cos(freqs.astype(np.float64)).astype(np.float32)
    s = np.sin(freqs.astype(np.float64)).astype(np.float32)
    tabC = np.ascontiguousarray(np.concatenate([c, c], axis=0))
    tabS = np.ascontiguousarray(np.concatenate([s, -s], axis=0))
    return tabC, tabS


def _tile_lhsT(wT, nblk):
    K_in = wT.shape[0]
    a = np.asarray(wT).reshape(K_in // 128, 128, nblk, 128)
    return np.ascontiguousarray(a.transpose(2, 1, 0, 3))


def host_prep(inp):
    perm = _perm_head()
    perm_q = np.concatenate([h * D + perm for h in range(NQ)])
    perm_k = np.concatenate([h * D + perm for h in range(NKV)])

    x = np.asarray(inp["L0_attn_x"], dtype=np.float32)
    WqT = np.ascontiguousarray(np.asarray(inp["L0_attn_Wq"], dtype=np.float32).T[:, perm_q])
    WkT = np.ascontiguousarray(np.asarray(inp["L0_attn_Wk"], dtype=np.float32).T[:, perm_k])
    WvT = np.ascontiguousarray(np.asarray(inp["L0_attn_Wv"], dtype=np.float32).T)
    WoT = np.ascontiguousarray(np.asarray(inp["L0_attn_Wo"], dtype=np.float32).T)

    shared = {
        "wq": _tile_lhsT(WqT, NCH),
        "wk": _tile_lhsT(WkT, NKV),
        "wv": np.ascontiguousarray(WvT.reshape(NCH, 128, NKV * 128).transpose(1, 0, 2)),
        "wo": _tile_lhsT(WoT, NCH),
        "la4": np.ascontiguousarray(np.stack(
            [np.asarray(inp[k], dtype=np.float32).T.reshape(NCH, 128, RANK).transpose(1, 0, 2).reshape(128, NCH * RANK)
             for k in ("L0_attn_qA", "L0_attn_kA", "L0_attn_vA", "L0_attn_oA")], axis=1)),
        "qb": np.ascontiguousarray(np.asarray(inp["L0_attn_qB"], dtype=np.float32).T[:, perm_q]),
        "kb": np.ascontiguousarray(np.asarray(inp["L0_attn_kB"], dtype=np.float32).T[:, perm_k]),
        "vb": np.ascontiguousarray(np.asarray(inp["L0_attn_vB"], dtype=np.float32).T),
        "ob": np.ascontiguousarray(np.asarray(inp["L0_attn_oB"], dtype=np.float32).T),
        "wg": _tile_lhsT(np.asarray(inp["L0_ffn_w_gate"]), NIB),
        "wu": _tile_lhsT(np.asarray(inp["L0_ffn_w_up"]), NIB),
        "wd": _tile_lhsT(np.asarray(inp["L0_ffn_w_down"]), NCH),
        "sg": np.ascontiguousarray(np.asarray(inp["L0_ffn_s_gate"], dtype=np.float32).reshape(NIB, 128).T),
        "su": np.ascontiguousarray(np.asarray(inp["L0_ffn_s_up"], dtype=np.float32).reshape(NIB, 128).T),
        "sd": np.ascontiguousarray(np.asarray(inp["L0_ffn_s_down"], dtype=np.float32).reshape(NCH, 128).T),
    }
    kc, ks = _rope_tables(np.arange(S))
    shared["ropekc"] = kc
    shared["ropeks"] = ks

    xbt = []
    for b in range(B):
        xbt.append(np.ascontiguousarray(x[b].T.reshape(NCH, 128, S)))

    in_maps, own_all = [], []
    for c in range(8):
        b, r = divmod(c, 4)
        blocks = [4 * (e - 1) + r for e in range(SLOTS, 0, -1)]
        own = np.concatenate([np.arange(j * 128, (j + 1) * 128) for j in blocks])
        own_all.append((b, own))
        xo = np.ascontiguousarray(x[b][own].T)
        qc, qs = _rope_tables(own)
        m = np.zeros((128, SLOTS, 4, 128), dtype=BF2)
        kk = np.arange(128)[:, None]
        qq = np.arange(128)[None, :]
        for sp in range(SLOTS):
            for jj in range(4):
                if jj < r:
                    m[:, sp, jj, :] = 1.0
                elif jj == r:
                    m[:, sp, jj, :] = (kk <= qq).astype(BF2)
        in_maps.append(dict(
            shared,
            xbt=xbt[b],
            xot=np.ascontiguousarray(xo.reshape(NCH, 128, T)),
            ropeqc=qc, ropeqs=qs,
            masks=np.ascontiguousarray(m.reshape(128, SLOTS * 4 * 128)),
        ))
    return in_maps, own_all


def build_program(taps=()):
    nc = bass.Bass("TRN2", target_bir_lowering=False)

    I = {}
    def din(name, shape, dtype):
        I[name] = nc.dram_tensor(name, list(shape), dtype, kind="ExternalInput").ap()

    din("xbt", [NCH, 128, S], F32)
    din("xot", [NCH, 128, T], F32)
    din("wq", [NCH, 128, NCH, 128], F32)
    din("wk", [NKV, 128, NCH, 128], F32)
    din("wv", [128, NCH, NKV * 128], F32)
    din("wo", [NCH, 128, NCH, 128], F32)
    din("la4", [128, 4, NCH * RANK], F32)
    din("qb", [RANK, NQ * 128], F32)
    din("kb", [RANK, NKV * 128], F32)
    din("vb", [RANK, NKV * 128], F32)
    din("ob", [RANK, NCH * 128], F32)
    din("wg", [NIB, 128, NCH, 128], I8)
    din("wu", [NIB, 128, NCH, 128], I8)
    din("wd", [NCH, 128, NIB, 128], I8)
    din("sg", [128, NIB], F32)
    din("su", [128, NIB], F32)
    din("sd", [128, NCH], F32)
    din("ropekc", [128, S], F32)
    din("ropeks", [128, S], F32)
    din("ropeqc", [128, T], F32)
    din("ropeqs", [128, T], F32)
    din("masks", [128, SLOTS * 4 * 128], BF16)

    yout = nc.dram_tensor("yout", [NCH, 128, T], F32, kind="ExternalOutput").ap()
    tk = lambda n: "ExternalOutput" if n in taps else "Internal"
    kspill = nc.dram_tensor("kspill", [NKV, 128, S], BF16, kind=tk("kspill")).ap()
    vspill = nc.dram_tensor("vspill", [2, 128, KB * 512], BF16, kind=tk("vspill")).ap()
    tapd = {}
    for nm in taps:
        if nm in ("kspill", "vspill"):
            continue
        if nm == "dbg":
            tapd[nm] = nc.dram_tensor("tap_dbg", [128, 64], F32, kind="ExternalOutput").ap()
            continue
        dt = BF16 if nm in ("qt", "ot") else F32
        tapd[nm] = nc.dram_tensor("tap_" + nm, [128, NCH * 1024], dt, kind="ExternalOutput").ap()

    AR = Arena(nc, 206 * 1024)
    DOF = 163840
    s32 = AR.tensor("s32", 0, [128, NCH * 1024], F32)
    s16 = AR.tensor("s16", 65536, [128, NCH * 1024], BF16)
    big = AR.tensor("big", 98304, [128, 32768], BF16)
    xcb = AR.tensor("xcb", 32768, [128, 2, NCH * 512], BF16)
    kvstage = AR.tensor("kvstage", 0, [128, 4, 512], BF16)
    xotb = AR.tensor("xotb", 98304, [128, NCH * 1024], BF16)

    def At(name, off, shape, dtype):
        return AR.tensor(name, DOF + off, list(shape), dtype)

    gpanb = At("gpanb", 0, [128, 2, 2048], BF16)
    wpanb = At("wpanb", 8192, [128, 2, 2048], BF16)
    ropec = At("ropec", 16384, [128, 1024], F32)
    ropes = At("ropes", 20480, [128, 1024], F32)
    qscr = At("qscr", 24576, [128, 1024], F32)
    qscr2 = At("qscr2", 28672, [128, 1024], F32)
    sqa = At("sqa", 32768, [RANK, 1024], BF16)
    sqsq = At("sqsq", 34816, [128, 1024], BF16)
    maskv = At("maskv", 16384, [128, SLOTS * 4 * 128], BF16)
    ptile = At("ptile", 24576, [128, 2, 1024], BF16)
    sbc = At("sbc", 28672, [128, 1024], F32)
    mubc = At("mubc", 16384, [128, 1024], F32)
    rstdbc = At("rstdbc", 20480, [128, 1024], F32)
    rowsA = At("rowsA", 24576, [1, 1024], F32)
    rowsB = At("rowsB", 28672, [1, 1024], F32)
    rowsC = At("rowsC", 32768, [1, 1024], F32)
    wdb = At("wdb", 16384, [128, 2, 4096], BF16)
    gbuf = At("gbuf", 32768, [128, 2, 512], BF16)
    upanb = At("upanb", 34816, [128, 2, 2048], BF16)
    lorab = At("lorab", 38912, [RANK, 2048], BF16)
    la4b = At("la4b", 43008, [128, 4, 128], BF16)
    onesb = At("onesb", 44032, [128, 1], BF16)
    ones32 = At("ones32", 44064, [1, 128], F32)
    negc = At("negc", 44576, [128, 16], F32)
    kmaxsq = At("kmaxsq", 44640, [1, 16], F32)
    qmaxsq = At("qmaxsq", 44704, [1, 8], F32)
    qmaxs2 = At("qmaxs2", 44736, [1, 8], F32)
    scrow = At("scrow", 44768, [1, 8], F32)
    scrow2 = At("scrow2", 44800, [1, 8], F32)
    sgc = At("sgc", 44832, [128, NIB], F32)
    suc = At("suc", 45088, [128, NIB], F32)
    sdc = At("sdc", 45344, [128, NCH], F32)
    cst = At("cst", 45408, [128, 4], F32)

    K = AR.keys

    p2a = nc.alloc_psum_tensor("p2a", [128, 1024], F32)
    p2b = nc.alloc_psum_tensor("p2b", [128, 1024], F32)
    p2c = nc.alloc_psum_tensor("p2c", [128, 1024], F32)
    prow = nc.alloc_psum_tensor("prow", [1, 1024], F32)
    PKEY = {"p2a": [("P", 0), ("P", 1)], "p2b": [("P", 2), ("P", 3)],
            "p2c": [("P", 4), ("P", 5)], "prow": [("P", 6), ("P", 7)]}

    def pkh(name, lo_el, hi_el):
        ks = PKEY[name]
        out = []
        if lo_el < 512:
            out.append(ks[0])
        if hi_el > 512:
            out.append(ks[1])
        return out

    sch = Sched()

    def MM(out_ap, lhsT_ap, rhs_ap, start, stop, reads, writes):
        sch.add("pe", lambda e, o=out_ap, l=lhsT_ap, r=rhs_ap, s1=start, s2=stop:
                e.matmul(o, l, r, start=s1, stop=s2), reads=reads, writes=writes)

    def DMA(eng, out_ap, in_ap, reads, writes, stream):
        return sch.add(eng, lambda e, o=out_ap, i=in_ap: e.dma_start(out=o, in_=i),
                       reads=reads, writes=writes, dma=True, stream=stream)

    def CAST(eng, out_ap, in_ap, reads, writes):
        if eng == "ac":
            sch.add(eng, lambda e, o=out_ap, i=in_ap: e.copy(out=o, in_=i),
                    reads=reads, writes=writes)
        else:
            sch.add(eng, lambda e, o=out_ap, i=in_ap: e.tensor_copy(out=o, in_=i),
                    reads=reads, writes=writes)

    def TT(eng, op, out_ap, a_ap, b_ap, reads, writes):
        sch.add(eng, lambda e, o=out_ap, a=a_ap, b=b_ap, al=op:
                e.tensor_tensor(out=o, in0=a, in1=b, op=al), reads=reads, writes=writes)

    def panel_ap(inp, j):
        return I[inp][j:j + 1].rearrange("a p c q -> p (a c) q").rearrange("p c q -> p (c q)")

    sch.add("dv", lambda e: e.memset(onesb[:], 1.0), writes=K("onesb"))
    sch.add("dv", lambda e: e.memset(ones32[:], 1.0), writes=K("ones32"))
    sch.add("dv", lambda e: e.memset(kmaxsq[:], 0.0), writes=K("kmaxsq"))
    sch.add("dv", lambda e: e.memset(cst[:, 0:1], EPS), writes=K("cst"))
    sch.add("dv", lambda e: e.memset(cst[:, 1:2], 1.0 / D), writes=K("cst"))
    sch.add("dv", lambda e: e.memset(cst[:, 2:3], SCALE), writes=K("cst"))
    DMA("sy", sgc[:], I["sg"], [], K("sgc"), "scg")
    DMA("sy", suc[:], I["su"], [], K("suc"), "scu")
    DMA("sy", sdc[:], I["sd"], [], K("sdc"), "scd")
    DMA("sy", ropec[:], I["ropeqc"], [], K("ropec"), "ropec")
    DMA("sy", ropes[:], I["ropeqs"], [], K("ropes"), "ropes")
    DMA("gp", la4b[:], I["la4"], K("la4b"), K("la4b"), "laA")
    laab = la4b[:, 0, :]
    laabK = la4b[:, 1, :]
    laab2 = la4b[:, 2, :]
    laabO = la4b[:, 3, :]
    DMA("gp", lorab[:], I["qb"], K("lorab"), K("lorab"), "lqb")
    DMA("gp", xotb[:].rearrange("p (c t) -> p c t", t=1024), I["xot"].rearrange("c p t -> p c t"),
        K("xotb"), K("xotb"), "xotb")

    for sH in range(2):
        for c in range(NCH):
            MM(p2b[0:RANK, sH * 512:(sH + 1) * 512], laab[:, c * RANK:(c + 1) * RANK],
               xotb[:, c * 1024 + sH * 512: c * 1024 + (sH + 1) * 512],
               c == 0, c == NCH - 1,
               K("la4b") + K("xotb", c * 1024 + sH * 512, c * 1024 + (sH + 1) * 512),
               pkh("p2b", sH * 512, (sH + 1) * 512))
    sch.add("dv", lambda e: e.tensor_scalar_mul(out=sqa[:], in0=p2b[0:RANK, :], scalar1=2.0),
            reads=PKEY["p2b"], writes=K("sqa"))

    DMA("gp", wpanb[:, 0, :], panel_ap("wq", 0), K("wpanb", 0, 2048), K("wpanb", 0, 2048), "panq0")
    DMA("gp", wpanb[:, 1, :], panel_ap("wq", 1), K("wpanb", 2048, 4096), K("wpanb", 2048, 4096), "panq1")
    pend_qnorms = []
    for bq in range(NQ):
        par = bq % 2
        wb = wpanb[:, par, :]
        wbk = K("wpanb", par * 2048, (par + 1) * 2048)
        pq, pqn = (p2a, "p2a") if par == 0 else (p2b, "p2b")
        for sH in range(2):
            sl = slice(sH * 512, (sH + 1) * 512)
            for c in range(NCH):
                MM(pq[:, sl], wb[:, c * 128:(c + 1) * 128],
                   xotb[:, c * 1024 + sH * 512: c * 1024 + (sH + 1) * 512],
                   c == 0, False,
                   wbk + K("xotb", c * 1024 + sH * 512, c * 1024 + (sH + 1) * 512),
                   pkh(pqn, sH * 512, (sH + 1) * 512))
            MM(pq[:, sl], lorab[0:RANK, bq * 128:(bq + 1) * 128], sqa[0:RANK, sl],
               False, True, K("lorab") + K("sqa"), pkh(pqn, sH * 512, (sH + 1) * 512))
        if bq + 2 < NQ:
            DMA("gp", wpanb[:, par, :], panel_ap("wq", bq + 2), wbk, wbk, "panq%d" % par)
        TT("dv", ALU.mult, qscr[:], pq[:, 0:1024], ropec[:], PKEY[pqn] + K("ropec"), K("qscr"))
        TT("dv", ALU.mult, qscr2[0:64, :], pq[64:128, 0:1024], ropes[64:128, :],
           PKEY[pqn] + K("ropes"), K("qscr2"))
        TT("dv", ALU.mult, qscr2[64:128, :], pq[0:64, 0:1024], ropes[0:64, :],
           PKEY[pqn] + K("ropes"), K("qscr2"))
        TT("dv", ALU.add, s16[:, bq * 1024:(bq + 1) * 1024], qscr[:], qscr2[:],
           K("qscr") + K("qscr2"), K("s16", bq * 1024, (bq + 1) * 1024))
        if len(pend_qnorms) >= 2:
            pend_qnorms.pop(0)()
        def qnorm(bq=bq):
            prwk = PKEY["prow"] if bq % 2 == 0 else PKEY["p2c"]
            sch.add("ac", lambda e, i=s16[:, bq * 1024:(bq + 1) * 1024], o=sqsq[:]:
                    e.activation(out=o, in_=i, func=AF.Square),
                    reads=K("s16", bq * 1024, (bq + 1) * 1024), writes=K("sqsq"))
            for sH in range(2):
                out = prow[0:1, sH * 512:(sH + 1) * 512] if bq % 2 == 0 else p2c[0:1, sH * 512:(sH + 1) * 512]
                MM(out, onesb[:], sqsq[:, sH * 512:(sH + 1) * 512],
                   True, True, K("onesb") + K("sqsq"), [prwk[sH]])
            qdst = qmaxsq if bq % 2 == 0 else qmaxs2
            src = prow[0:1, 0:1024] if bq % 2 == 0 else p2c[0:1, 0:1024]
            sch.add("dv", lambda e, o=qdst[0:1, bq // 2:bq // 2 + 1], i=src:
                    e.reduce_max(out=o, in_=i, axis=AXX),
                    reads=prwk, writes=K("qmaxsq") + K("qmaxs2"))
        pend_qnorms.append(qnorm)
    while pend_qnorms:
        pend_qnorms.pop(0)()
    if "qt" in tapd:
        DMA("gp", tapd["qt"], s16[:], K("s16"), [], "tapqt")

    DMA("gp", lorab[0:RANK, 0:1024], I["kb"], K("lorab", 0, 1024), K("lorab", 0, 1024), "lkb")
    DMA("gp", lorab[0:RANK, 1024:2048], I["vb"], K("lorab", 1024, 2048), K("lorab", 1024, 2048), "lvb")
    for kh in range(NKV):
        DMA("gp", big[:, kh * 2048:(kh + 1) * 2048], panel_ap("wk", kh),
            K("big", kh * 2048, (kh + 1) * 2048), K("big", kh * 2048, (kh + 1) * 2048), "wk%d" % kh)
    for cp in range(NCH // 2):
        DMA("gp", big[:, 16384 + cp * 2048:16384 + (cp + 1) * 2048].rearrange("p (a t) -> p a t", a=2),
            I["wv"][:, 2 * cp:2 * cp + 2, :],
            K("big", 16384 + cp * 2048, 16384 + (cp + 1) * 2048),
            K("big", 16384 + cp * 2048, 16384 + (cp + 1) * 2048), "wv%d" % cp)
    wkb = big[:, 0:16384]
    wvb = big[:, 16384:32768]

    pend_knorms = []
    for tc in range(TC):
        tsl = slice(tc * 512, (tc + 1) * 512)
        par = tc % 2
        xv = xcb[:, par, :]
        xvk = K("xcb", par * NCH * 512, (par + 1) * NCH * 512)
        DMA("gp", xv.rearrange("p (c t) -> p c t", t=512),
            I["xbt"][:, :, tsl].rearrange("c p t -> p c t"), xvk, xvk, "xcb%d" % par)
        DMA("sy", ropec[:, 0:512], I["ropekc"][:, tsl], K("ropec"), K("ropec", 0, 512), "ropec")
        DMA("sy", ropes[:, 0:512], I["ropeks"][:, tsl], K("ropes"), K("ropes", 0, 512), "ropes")
        for c in range(NCH):
            MM(p2b[0:RANK, 0:512], laabK[:, c * RANK:(c + 1) * RANK], xv[:, c * 512:(c + 1) * 512],
               c == 0, c == NCH - 1, K("la4b") + xvk, pkh("p2b", 0, 512))
        sch.add("dv", lambda e: e.tensor_scalar_mul(out=sqa[0:RANK, 0:512], in0=p2b[0:RANK, 0:512], scalar1=2.0),
                reads=pkh("p2b", 0, 512), writes=K("sqa", 0, 512))
        for c in range(NCH):
            MM(p2b[0:RANK, 512:1024], laab2[:, c * RANK:(c + 1) * RANK], xv[:, c * 512:(c + 1) * 512],
               c == 0, c == NCH - 1, K("la4b") + xvk, pkh("p2b", 512, 1024))
        sch.add("dv", lambda e: e.tensor_scalar_mul(out=sqa[0:RANK, 512:1024], in0=p2b[0:RANK, 512:1024], scalar1=2.0),
                reads=pkh("p2b", 512, 1024), writes=K("sqa", 512, 1024))
        ska, skak = sqa[0:RANK, 0:512], K("sqa", 0, 512)
        sva, svak = sqa[0:RANK, 512:1024], K("sqa", 512, 1024)
        for kh in range(NKV):
            kp = kh % 2
            pk = p2a[:, kp * 512:(kp + 1) * 512]
            pkk = pkh("p2a", kp * 512, (kp + 1) * 512)
            for c in range(NCH):
                MM(pk, wkb[:, kh * 2048 + c * 128: kh * 2048 + (c + 1) * 128],
                   xv[:, c * 512:(c + 1) * 512], c == 0, False,
                   K("big", kh * 2048, (kh + 1) * 2048) + xvk, pkk)
            MM(pk, lorab[0:RANK, kh * 128:(kh + 1) * 128], ska, False, True,
               K("lorab", 0, 1024) + skak, pkk)
            if len(pend_knorms) >= 2:
                pend_knorms.pop(0)()
            kst = kvstage[:, kh % 4, :]
            kstk = K("kvstage", (kh % 4) * 512, (kh % 4 + 1) * 512)
            TT("dv", ALU.mult, qscr[:, 0:512], pk, ropec[:, 0:512], pkk + K("ropec", 0, 512), K("qscr", 0, 512))
            TT("dv", ALU.mult, qscr2[0:64, 0:512], pk[64:128, :], ropes[64:128, 0:512],
               pkk + K("ropes", 0, 512), K("qscr2", 0, 512))
            TT("dv", ALU.mult, qscr2[64:128, 0:512], pk[0:64, :], ropes[0:64, 0:512],
               pkk + K("ropes", 0, 512), K("qscr2", 0, 512))
            TT("dv", ALU.add, kst, qscr[:, 0:512], qscr2[:, 0:512],
               K("qscr", 0, 512) + K("qscr2", 0, 512), kstk)
            DMA("sy", kspill[kh:kh + 1, :, tsl].rearrange("a p t -> p (a t)"),
                kst, kstk, [("KS", kh, tc)], "ks")
            def knorm(kh=kh, kst=kst, kstk=kstk):
                kq = (kh % 2) * 512
                sch.add("ac", lambda e: e.activation(out=sqsq[:, kq:kq + 512], in_=kst, func=AF.Square),
                        reads=kstk, writes=K("sqsq", kq, kq + 512))
                MM(prow[0:1, kq:kq + 512], onesb[:], sqsq[:, kq:kq + 512], True, True,
                   K("onesb") + K("sqsq", kq, kq + 512), pkh("prow", kq, kq + 512))
                sch.add("dv", lambda e, o=scrow2[0:1, 0:1] if kh % 2 else scrow[0:1, 0:1]:
                        e.reduce_max(out=o, in_=prow[0:1, kq:kq + 512], axis=AXX),
                        reads=pkh("prow", kq, kq + 512), writes=K("scrow") + K("scrow2"))
                sch.add("dv", lambda e, o=kmaxsq[0:1, kh:kh + 1],
                        i=scrow2[0:1, 0:1] if kh % 2 else scrow[0:1, 0:1]:
                        e.tensor_tensor(out=o, in0=i, in1=o, op=ALU.max),
                        reads=K("scrow") + K("scrow2") + K("kmaxsq"), writes=K("kmaxsq"))
            pend_knorms.append(knorm)
        while pend_knorms:
            pend_knorms.pop(0)()
        for tb in range(4):
            for vq in range(2):
                vp = (tb * 2 + vq) % 2
                pv = p2c[:, vp * 512:(vp + 1) * 512]
                pvk = pkh("p2c", vp * 512, (vp + 1) * 512)
                for c in range(NCH):
                    MM(pv, xv[:, c * 512 + tb * 128: c * 512 + (tb + 1) * 128],
                       wvb[:, c * 1024 + vq * 512: c * 1024 + (vq + 1) * 512],
                       c == 0, False,
                       xvk + K("big", 16384 + c * 1024 + vq * 512, 16384 + c * 1024 + (vq + 1) * 512), pvk)
                MM(pv, sva[0:RANK, tb * 128:(tb + 1) * 128],
                   lorab[0:RANK, 1024 + vq * 512: 1024 + (vq + 1) * 512],
                   False, True, svak + K("lorab", 1024, 2048), pvk)
                CAST("dv", sqsq[:, 0:512], pv, pvk, K("sqsq", 0, 512))
                DMA("sy", vspill[vq:vq + 1, :, (tc * 4 + tb) * 512:(tc * 4 + tb + 1) * 512]
                    .rearrange("a p t -> p (a t)"),
                    sqsq[:, 0:512], K("sqsq", 0, 512), [("VS", vq, tc * 4 + tb)], "vs")

    for qsrc, sdst in ((qmaxsq, scrow), (qmaxs2, scrow2)):
        sch.add("dv", lambda e, a=qsrc, o=sdst: e.tensor_tensor(
                out=o[0:1, 0:8], in0=a[0:1, 0:8], in1=kmaxsq[0:1, 0:8], op=ALU.mult),
                reads=K("qmaxsq") + K("qmaxs2") + K("kmaxsq"), writes=K("scrow") + K("scrow2"))
        sch.add("ac", lambda e, o=sdst: e.activation(out=o[0:1, 0:8], in_=o[0:1, 0:8],
                                                     func=AF.Sqrt, scale=cst[0:1, 1:2]),
                reads=K("scrow") + K("scrow2") + K("cst"), writes=K("scrow") + K("scrow2"))
        sch.add("dv", lambda e, o=sdst: e.tensor_scalar_mul(out=o[0:1, 0:8], in0=o[0:1, 0:8],
                                                            scalar1=-1.0),
                reads=K("scrow") + K("scrow2"), writes=K("scrow") + K("scrow2"))
    MM(p2c[:, 0:8], ones32[:], scrow[0:1, 0:8], True, True,
       K("ones32") + K("scrow"), pkh("p2c", 0, 8))
    MM(p2c[:, 8:16], ones32[:], scrow2[0:1, 0:8], True, True,
       K("ones32") + K("scrow2"), pkh("p2c", 0, 16))
    sch.add("dv", lambda e: e.tensor_copy(out=negc[:], in_=p2c[:, 0:16]),
            reads=pkh("p2c", 0, 16), writes=K("negc"))
    if "dbg" in tapd:
        DMA("gp", tapd["dbg"][:, 0:16], negc[:], K("negc"), [], "tapd0")
        DMA("gp", tapd["dbg"][0:1, 16:32], kmaxsq[:], K("kmaxsq"), [], "tapd1")
        DMA("gp", tapd["dbg"][0:1, 32:40], qmaxsq[:], K("qmaxsq"), [], "tapd2")
        DMA("gp", tapd["dbg"][0:1, 40:48], qmaxs2[:], K("qmaxs2"), [], "tapd3")

    DMA("sy", maskv[:], I["masks"], K("maskv"), K("maskv"), "mask")
    DMA("sy", s32[:].rearrange("p (c t) -> p c t", t=T), I["xot"].rearrange("c p t -> p c t"),
        K("s32"), K("s32"), "xot")
    DMA("gp", lorab[0:RANK, :], I["ob"], K("lorab"), K("lorab"), "lob")

    OT = big[:, 0:16384]
    stv = [0]

    def st_step(kh, h, kb):
        kp = kh % 2
        kbuf = big[:, 16384 + kp * 4096: 16384 + (kp + 1) * 4096]
        kbk = K("big", 16384 + kp * 4096, 16384 + (kp + 1) * 4096)
        n = 8 - kb // 4
        nw = n * 128
        stp = stv[0]
        pst, pstn = (p2a, "p2a") if stp == 0 else (p2b, "p2b")
        for s0 in range(0, nw, 512):
            s1 = min(s0 + 512, nw)
            MM(pst[:, s0:s1], kbuf[:, kb * 128:(kb + 1) * 128],
               s16[:, h * 1024 + s0: h * 1024 + s1], True, True,
               kbk + K("s16", h * 1024 + s0, h * 1024 + s1), pkh(pstn, s0, s1))
        stv[0] ^= 1
        return pst, pstn, n, nw

    def fin_step(kh, h, kb, pst, pstn, n, nw):
        kp = kh % 2
        vbuf = big[:, 24576 + kp * 4096: 24576 + (kp + 1) * 4096]
        vbk = K("big", 24576 + kp * 4096, 24576 + (kp + 1) * 4096)
        pp = kb % 2
        ptv = ptile[:, pp, :]
        qp = (h % 2) * 8 + h // 2
        rd = pkh(pstn, 0, nw) + K("negc") + K("cst")
        sch.add("ac", lambda e, o=ptv[:, 0:nw], i=pst[:, 0:nw], bi=negc[:, qp:qp + 1]:
                e.activation(out=o, in_=i, func=AF.Exp, bias=bi, scale=cst[:, 2:3]),
                reads=rd, writes=K("ptile", pp * 1024, pp * 1024 + nw))
        sp, jj = kb // 4, kb % 4
        mcol = (sp * 4 + jj) * 128
        TT("dv", ALU.mult, ptv[:, nw - 128:nw], ptv[:, nw - 128:nw],
           maskv[:, mcol:mcol + 128],
           K("ptile", pp * 1024 + nw - 128, pp * 1024 + nw) + K("maskv", mcol, mcol + 128),
           K("ptile", pp * 1024 + nw - 128, pp * 1024 + nw))
        for s0 in range(0, nw, 512):
            s1 = min(s0 + 512, nw)
            MM(prow[0:1, s0:s1], onesb[:], ptv[:, s0:s1],
               kb == 0, (kb == KB - 1) if s0 == 0 else (kb == 15),
               K("onesb") + K("ptile", pp * 1024 + s0, pp * 1024 + s1),
               pkh("prow", s0, s1))
            MM(p2c[:, s0:s1], vbuf[:, kb * 128:(kb + 1) * 128], ptv[:, s0:s1],
               kb == 0, (kb == KB - 1) if s0 == 0 else (kb == 15),
               vbk + K("ptile", pp * 1024 + s0, pp * 1024 + s1),
               pkh("p2c", s0, s1))

    def head_tail(h):
        sch.add("ac", lambda e: e.activation(out=sbc[0:1, 0:1024], in_=prow[0:1, 0:1024],
                                             func=AF.Ln),
                reads=PKEY["prow"], writes=K("sbc", 0, 1024))
        sch.add("ac", lambda e: e.activation(out=sbc[0:1, 0:1024], in_=sbc[0:1, 0:1024],
                                             func=AF.Exp, scale=-1.0),
                reads=K("sbc", 0, 1024), writes=K("sbc", 0, 1024))
        stp = stv[0]
        bps, bkn = (p2a, "p2a") if stp == 0 else (p2b, "p2b")
        for s0 in (0, 512):
            MM(bps[:, s0:s0 + 512], ones32[:], sbc[0:1, s0:s0 + 512], True, True,
               K("ones32") + K("sbc", 0, 1024), pkh(bkn, s0, s0 + 512))
        sch.add("ac", lambda e, p=bps: e.copy(out=sbc[:], in_=p[:, 0:1024]),
                reads=PKEY[bkn], writes=K("sbc"))
        sch.add("dv", lambda e, o=OT[:, h * 1024:(h + 1) * 1024]:
                e.tensor_tensor(out=o, in0=p2c[:, 0:1024], in1=sbc[:], op=ALU.mult),
                reads=PKEY["p2c"] + K("sbc"), writes=K("big", h * 1024, (h + 1) * 1024))

    steps = []
    for kh in range(NKV):
        for h in (2 * kh, 2 * kh + 1):
            for kb in range(KB):
                steps.append((kh, h, kb))
    pend = []
    for kh in range(NKV):
        kp = kh % 2
        kbuf = big[:, 16384 + kp * 4096: 16384 + (kp + 1) * 4096]
        kbk = K("big", 16384 + kp * 4096, 16384 + (kp + 1) * 4096)
        vbuf = big[:, 24576 + kp * 4096: 24576 + (kp + 1) * 4096]
        vbk = K("big", 24576 + kp * 4096, 24576 + (kp + 1) * 4096)
        DMA("gp", kbuf, kspill[kh:kh + 1].rearrange("a p t -> p (a t)"),
            [("KS", kh, t_) for t_ in range(TC)] + kbk, kbk, "kbuf%d" % kp)
        vq, w = kh // 4, kh % 4
        vsrc = (vspill[vq:vq + 1].rearrange("a p t -> p (a t)")
                .rearrange("p (tb d) -> p tb d", d=512)[:, :, w * 128:(w + 1) * 128])
        DMA("gp", vbuf.rearrange("p (tb d) -> p tb d", d=128), vsrc,
            [("VS", vq, t_) for t_ in range(KB)] + vbk, vbk, "vbuf%d" % kp)
        for h in (2 * kh, 2 * kh + 1):
            for kb in range(KB):
                ctx = st_step(kh, h, kb)
                if pend:
                    fn = pend.pop(0)
                    fn()
                if kb < KB - 1:
                    pend.append(lambda kh=kh, h=h, kb=kb, c=ctx: fin_step(kh, h, kb, *c))
                else:
                    def last(kh=kh, h=h, kb=kb, c=ctx):
                        fin_step(kh, h, kb, *c)
                        head_tail(h)
                    pend.append(last)
    for fn in pend:
        fn()
    if "ot" in tapd:
        DMA("gp", tapd["ot"], OT, K("big", 0, 16384), [], "tapot")

    def ln_stats_chunk(c, sqp="p2c"):
        csl = slice(c * 1024, (c + 1) * 1024)
        ck = K("s16", c * 1024, (c + 1) * 1024)
        par = c % 2
        sq = gpanb[:, par, 0:1024]
        xk = K("gpanb", par * 2048, par * 2048 + 1024)
        CAST("dv", s16[:, csl], s32[:, csl], K("s32", c * 1024, (c + 1) * 1024), ck)
        sch.add("ac", lambda e, i=s16[:, csl], o=sq:
                e.activation(out=o, in_=i, func=AF.Square),
                reads=ck, writes=xk)
        for sH in range(2):
            MM(prow[0:1, sH * 512:(sH + 1) * 512], onesb[:],
               s16[:, c * 1024 + sH * 512: c * 1024 + (sH + 1) * 512],
               c == 0, c == NCH - 1,
               K("onesb") + ck, [PKEY["prow"][sH]])
            sqt = p2c if sqp == "p2c" else p2b
            MM(sqt[0:1, sH * 512:(sH + 1) * 512], onesb[:],
               sq[:, sH * 512:(sH + 1) * 512],
               c == 0, c == NCH - 1,
               K("onesb") + xk, [PKEY[sqp][sH]])

    def ln_finish(emit_bf16, sqp="p2c"):
        sqt = p2c if sqp == "p2c" else p2b
        sch.add("dv", lambda e: e.tensor_scalar_mul(out=rowsA[:], in0=prow[0:1, :], scalar1=1.0 / H),
                reads=PKEY["prow"], writes=K("rowsA"))
        sch.add("dv", lambda e, i=sqt: e.tensor_scalar_mul(out=rowsB[:], in0=i[0:1, :], scalar1=1.0 / H),
                reads=PKEY[sqp], writes=K("rowsB"))
        TT("dv", ALU.mult, rowsC[:], rowsA[:], rowsA[:], K("rowsA"), K("rowsC"))
        TT("dv", ALU.subtract, rowsB[:], rowsB[:], rowsC[:], K("rowsB") + K("rowsC"), K("rowsB"))
        sch.add("ac", lambda e: e.activation(out=rowsB[:], in_=rowsB[:], func=AF.Sqrt,
                                             bias=cst[0:1, 0:1]),
                reads=K("rowsB") + K("cst"), writes=K("rowsB"))
        sch.add("ac", lambda e: e.activation(out=rowsC[:], in_=rowsB[:], func=AF.Ln),
                reads=K("rowsB"), writes=K("rowsC"))
        sch.add("ac", lambda e: e.activation(out=rowsC[:], in_=rowsC[:], func=AF.Exp, scale=-1.0),
                reads=K("rowsC"), writes=K("rowsC"))
        for s0 in (0, 512):
            MM(p2a[:, s0:s0 + 512], ones32[:], rowsA[0:1, s0:s0 + 512], True, True,
               K("ones32") + K("rowsA"), pkh("p2a", s0, s0 + 512))
            MM(p2b[:, s0:s0 + 512], ones32[:], rowsC[0:1, s0:s0 + 512], True, True,
               K("ones32") + K("rowsC"), pkh("p2b", s0, s0 + 512))
        sch.add("dv", lambda e: e.tensor_copy(out=mubc[:], in_=p2a[:, 0:1024]),
                reads=PKEY["p2a"], writes=K("mubc"))
        sch.add("ac", lambda e: e.copy(out=rstdbc[:], in_=p2b[:, 0:1024]),
                reads=PKEY["p2b"], writes=K("rstdbc"))
        for c in range(NCH):
            csl = slice(c * 1024, (c + 1) * 1024)
            sk = K("s32", c * 1024, (c + 1) * 1024)
            TT("dv", ALU.subtract, s32[:, csl], s32[:, csl], mubc[:], sk + K("mubc"), sk)
            TT("dv", ALU.mult, s32[:, csl], s32[:, csl], rstdbc[:], sk + K("rstdbc"), sk)
            if emit_bf16:
                CAST("dv", s16[:, csl], s32[:, csl], sk, K("s16", c * 1024, (c + 1) * 1024))

    for sH in range(2):
        for c in range(NCH):
            MM(p2b[0:RANK, sH * 512:(sH + 1) * 512], laabO[:, c * RANK:(c + 1) * RANK],
               OT[:, c * 1024 + sH * 512: c * 1024 + (sH + 1) * 512],
               c == 0, c == NCH - 1,
               K("la4b") + K("big", c * 1024 + sH * 512, c * 1024 + (sH + 1) * 512),
               pkh("p2b", sH * 512, (sH + 1) * 512))
    sch.add("dv", lambda e: e.tensor_scalar_mul(out=sqa[:], in0=p2b[0:RANK, :], scalar1=2.0),
            reads=PKEY["p2b"], writes=K("sqa"))
    DMA("gp", wpanb[:, 0, :], panel_ap("wo", 0), K("wpanb", 0, 2048), K("wpanb", 0, 2048), "pano0")
    DMA("gp", wpanb[:, 1, :], panel_ap("wo", 1), K("wpanb", 2048, 4096), K("wpanb", 2048, 4096), "pano1")
    for ob in range(NCH):
        par = ob % 2
        wb = wpanb[:, par, :]
        wbk = K("wpanb", par * 2048, (par + 1) * 2048)
        pw, pwn = (p2a, "p2a") if par == 0 else (p2b, "p2b")
        for sH in range(2):
            sl = slice(sH * 512, (sH + 1) * 512)
            for c in range(NCH):
                MM(pw[:, sl], wb[:, c * 128:(c + 1) * 128],
                   OT[:, c * 1024 + sH * 512: c * 1024 + (sH + 1) * 512], c == 0, False,
                   wbk + K("big", c * 1024 + sH * 512, c * 1024 + (sH + 1) * 512),
                   pkh(pwn, sH * 512, (sH + 1) * 512))
            MM(pw[:, sl], lorab[0:RANK, ob * 128:(ob + 1) * 128], sqa[0:RANK, sl],
               False, True, K("lorab") + K("sqa"), pkh(pwn, sH * 512, (sH + 1) * 512))
        if ob + 2 < NCH:
            DMA("gp", wpanb[:, par, :], panel_ap("wo", ob + 2), wbk, wbk, "pano%d" % par)
        sch.add("dv", lambda e, o=s32[:, ob * 1024:(ob + 1) * 1024], p=pw:
                e.tensor_tensor(out=o, in0=p[:, 0:1024], in1=o, op=ALU.add),
                reads=PKEY[pwn] + K("s32", ob * 1024, (ob + 1) * 1024),
                writes=K("s32", ob * 1024, (ob + 1) * 1024))
        ln_stats_chunk(ob)
    if "r1" in tapd:
        DMA("gp", tapd["r1"], s32[:], K("s32"), [], "tapr1")

    ln_finish(True)
    if "xln" in tapd:
        DMA("gp", tapd["xln"], s32[:], K("s32"), [], "tapxln")

    fuse = big[:, 0:32768]
    for s2 in range(2):
        toff = s2 * 512
        for ib in range(NIB):
            par = ib % 2
            gb = gpanb[:, par, :]
            ub = upanb[:, par, :]
            gbk = K("gpanb", par * 2048, (par + 1) * 2048)
            ubk = K("upanb", par * 2048, (par + 1) * 2048)
            gib = s2 * NIB + ib
            if gib < 2:
                DMA("gp", gb.rearrange("p (c q) -> p c q", q=128),
                    I["wg"][ib:ib + 1].rearrange("a p c q -> p (a c) q"), gbk, gbk, "gp8%d" % par)
                DMA("gp", ub.rearrange("p (c q) -> p c q", q=128),
                    I["wu"][ib:ib + 1].rearrange("a p c q -> p (a c) q"), ubk, ubk, "up8%d" % par)
            pg = p2a[:, par * 512:(par + 1) * 512]
            pgk = pkh("p2a", par * 512, (par + 1) * 512)
            pu2 = p2b[:, par * 512:(par + 1) * 512]
            puk = pkh("p2b", par * 512, (par + 1) * 512)
            for c in range(NCH):
                MM(pg, gb[:, c * 128:(c + 1) * 128],
                   s16[:, c * 1024 + toff: c * 1024 + toff + 512], c == 0, c == NCH - 1,
                   gbk + K("s16", c * 1024 + toff, c * 1024 + toff + 512), pgk)
            for c in range(NCH):
                MM(pu2, ub[:, c * 128:(c + 1) * 128],
                   s16[:, c * 1024 + toff: c * 1024 + toff + 512], c == 0, c == NCH - 1,
                   ubk + K("s16", c * 1024 + toff, c * 1024 + toff + 512), puk)
            nx = gib + 2
            if nx < 2 * NIB:
                nib_, npar = nx % NIB, nx % 2
                ngb = gpanb[:, npar, :]
                nub = upanb[:, npar, :]
                ngbk = K("gpanb", npar * 2048, (npar + 1) * 2048)
                nubk = K("upanb", npar * 2048, (npar + 1) * 2048)
                DMA("gp", ngb.rearrange("p (c q) -> p c q", q=128),
                    I["wg"][nib_:nib_ + 1].rearrange("a p c q -> p (a c) q"), ngbk, ngbk, "gp8%d" % npar)
                DMA("gp", nub.rearrange("p (c q) -> p c q", q=128),
                    I["wu"][nib_:nib_ + 1].rearrange("a p c q -> p (a c) q"), nubk, nubk, "up8%d" % npar)
            gv = gbuf[:, par, :]
            gvk = K("gbuf", par * 512, (par + 1) * 512)
            sch.add("ac", lambda e, o=gv, i=pg, sc=sgc[:, ib:ib + 1]:
                    e.activation(out=o, in_=i, func=AF.Silu, scale=sc),
                    reads=pgk + K("sgc"), writes=gvk)
            sch.add("dv", lambda e, o=fuse[:, ib * 512:(ib + 1) * 512], i=pu2, sc=suc[:, ib:ib + 1], gg=gv:
                    e.scalar_tensor_tensor(out=o, in0=i, scalar=sc, in1=gg, op0=ALU.mult, op1=ALU.mult),
                    reads=puk + K("suc") + gvk, writes=K("big", ib * 512, (ib + 1) * 512))
        for ob in range(NCH):
            pd = p2c[:, (ob % 2) * 512:((ob % 2) + 1) * 512]
            pdk = pkh("p2c", (ob % 2) * 512, ((ob % 2) + 1) * 512)
            for half in range(2):
                hp = (ob * 2 + half + s2) % 2
                dv_ = wdb[:, hp, :]
                dvk = K("wdb", hp * 4096, (hp + 1) * 4096)
                DMA("gp", dv_.rearrange("p (c q) -> p c q", q=128),
                    I["wd"][ob:ob + 1].rearrange("a p c q -> p (a c) q")[:, half * 32:(half + 1) * 32, :],
                    dvk, dvk, "wd%d" % hp)
                for u in range(32):
                    c = half * 32 + u
                    MM(pd, dv_[:, u * 128:(u + 1) * 128], fuse[:, c * 512:(c + 1) * 512],
                       c == 0, c == NIB - 1,
                       dvk + K("big", c * 512, (c + 1) * 512), pdk)
            sch.add("dv", lambda e, o=s32[:, ob * 1024 + toff: ob * 1024 + toff + 512],
                    i=pd, sc=sdc[:, ob:ob + 1]:
                    e.scalar_tensor_tensor(out=o, in0=i, scalar=sc, in1=o, op0=ALU.mult, op1=ALU.add),
                    reads=pdk + K("sdc") + K("s32", ob * 1024 + toff, ob * 1024 + toff + 512),
                    writes=K("s32", ob * 1024 + toff, ob * 1024 + toff + 512))
            if s2 == 1:
                ln_stats_chunk(ob, sqp="p2b")
    if "r2" in tapd:
        DMA("gp", tapd["r2"], s32[:], K("s32"), [], "tapr2")

    ln_finish(False, sqp="p2b")
    out_dma = DMA("gp", yout.rearrange("c p t -> p c t"),
                  s32[:].rearrange("p (c t) -> p c t", t=T), K("s32"), [], "yout")
    sch.add("gp", lambda e: e.engine_nop(), deps=(out_dma,))

    cnt = sch.finalize()
    nc._sched = sch
    with ExitStack() as stack:
        E = stack.enter_context
        sems = {}
        for key in sorted(cnt):
            sems[key] = E(nc.semaphore("sem_" + key.replace(":", "_")))
        block = E(nc.Block())

        @block.tensor
        def _(eng):
            sch.emit("pe", eng, sems)

        @block.vector
        def _(eng):
            sch.emit("dv", eng, sems)

        @block.scalar
        def _(eng):
            sch.emit("ac", eng, sems)

        @block.sync
        def _(eng):
            sch.emit("sy", eng, sems)

        @block.gpsimd
        def _(eng):
            sch.emit("gp", eng, sems)

    return nc


def kernel(**inputs):
    in_maps, own_all = host_prep(inputs)
    nc = build_program()
    res = run_bass_kernel_spmd(nc, in_maps, core_ids=list(range(8)))
    y = np.zeros((B, S, H), dtype=np.float32)
    for c in range(8):
        b, own = own_all[c]
        yc = np.asarray(res.results[c]["yout"])
        y[b, own] = yc.reshape(H, T).T
    return y


def run_dev(inputs, taps=()):
    in_maps, own_all = host_prep(inputs)
    nc = build_program(taps=taps)
    res = run_bass_kernel_spmd(nc, in_maps, core_ids=list(range(8)))
    return res, own_all


# version 10
# speedup vs baseline: 1.0348x; 1.0348x over previous
import math
import os
from types import SimpleNamespace

import numpy as np
import ml_dtypes

import concourse.bass as bass
import concourse.mybir as mybir
from concourse.bass_utils import run_bass_kernel_spmd
from contextlib import ExitStack

F32 = mybir.dt.float32
BF16 = mybir.dt.bfloat16
I8 = mybir.dt.int8
AF = mybir.ActivationFunctionType
ALU = mybir.AluOpType
AXX = mybir.AxisListType.X

B, S, H = 2, 4096, 2048
NQ, NKV, D, RANK = 16, 8, 128, 8
INTER = 8192
NCH = H // 128
SLOTS = 8
T = SLOTS * 128
KB = S // 128
TC = 8
NIB = INTER // 128
SCALE = 1.0 / math.sqrt(D)
EPS = 1e-5

PAGE = 2048
BF2 = np.dtype(ml_dtypes.bfloat16)
SAFE_SYNC = os.environ.get("BASS_SIM_FP64") == "1"


class Sched:
    ENGS = ("pe", "dv", "ac", "sy", "gp")

    def __init__(self):
        self.ops = []
        self.lastw = {}
        self.readers = {}

    def add(self, eng, fn, reads=(), writes=(), dma=False, deps=(), stream=None, tag=""):
        op = SimpleNamespace(i=len(self.ops), eng=eng, fn=fn, dma=dma,
                             deps=set(), need_inc=False, cum=0, clock=None,
                             semkey=None, waits=None, stream=stream, tag=tag)
        for k in reads:
            w = self.lastw.get(k)
            if w is not None:
                op.deps.add(w.i)
            self.readers.setdefault(k, []).append(op)
        for k in writes:
            w = self.lastw.get(k)
            if w is not None:
                op.deps.add(w.i)
            for rop in self.readers.get(k, ()):
                op.deps.add(rop.i)
            self.lastw[k] = op
            self.readers[k] = []
        for d in deps:
            op.deps.add(d.i)
        op.deps.discard(op.i)
        self.ops.append(op)
        return op

    def finalize(self):
        for op in self.ops:
            if op.dma:
                op.need_inc = True
        for op in self.ops:
            for d in op.deps:
                dop = self.ops[d]
                if dop.dma or dop.eng != op.eng or SAFE_SYNC:
                    dop.need_inc = True
        cnt = {}
        for op in self.ops:
            op.semkey = ("D:" + (op.stream or op.eng)) if op.dma else op.eng
            if op.need_inc:
                cnt[op.semkey] = cnt.get(op.semkey, 0) + (16 if op.dma else 1)
                op.cum = cnt[op.semkey]
        for op in self.ops:
            cl = {}
            for d in op.deps:
                for k, v in self.ops[d].clock.items():
                    if cl.get(k, 0) < v:
                        cl[k] = v
            if op.need_inc and op.cum:
                cl[op.semkey] = op.cum
            op.clock = cl
        eclock = {e: {} for e in self.ENGS}
        self.per_eng = {e: [] for e in self.ENGS}
        for op in self.ops:
            ec = eclock[op.eng]
            waits = []
            for d in sorted(op.deps):
                dop = self.ops[d]
                if dop.eng == op.eng and not dop.dma and not SAFE_SYNC:
                    continue
                if dop.cum and ec.get(dop.semkey, 0) < dop.cum:
                    waits.append((dop.semkey, dop.cum))
                    for k, v in dop.clock.items():
                        if ec.get(k, 0) < v:
                            ec[k] = v
            op.waits = waits
            self.per_eng[op.eng].append(op)
        return cnt

    def emit(self, eng_name, engobj, sems):
        for op in self.per_eng[eng_name]:
            for sk, val in op.waits:
                engobj.wait_ge(sems[sk], val)
            inst = op.fn(engobj)
            if op.need_inc:
                assert inst is not None, f"op {op.i} on {op.eng} needs inc but returned no instruction"
                inst.then_inc(sems[op.semkey], 16 if op.dma else 1)


class Arena:

    def __init__(self, nc, nbytes):
        self.nc = nc
        self.nbytes = nbytes
        self.handle = nc.alloc_sbuf_tensor("arena", [128, nbytes], mybir.dt.uint8)
        self.base = nc.lookup_mloc(self.handle).addr
        self.tensors = {}

    def tensor(self, name, off, shape, dtype):
        size = int(np.prod(shape[1:])) * mybir.dt.size(dtype)
        assert off % 32 == 0 and off + size <= self.nbytes, (name, off, size, self.nbytes)
        h = self.nc.virtual_sbuf_tensor(name, list(shape), dtype)
        self.nc._tensor(h.name, list(shape), dtype, type="SB")
        mloc = self.nc.lookup_mloc(h)
        mloc.allocated = not self.nc.target_bir_lowering
        mloc.addr = self.base + off
        self.tensors[name] = (h, off, size)
        return h

    def keys(self, name, el_lo=None, el_hi=None):
        h, off, size = self.tensors[name]
        es = mybir.dt.size(h.dtype)
        lo = off + (0 if el_lo is None else el_lo * es)
        hi = off + (size if el_hi is None else el_hi * es)
        return [("S", p) for p in range(lo // PAGE, (hi + PAGE - 1) // PAGE)]


def _perm_head():
    return np.concatenate([np.arange(0, D, 2), np.arange(1, D, 2)])


def _rope_tables(positions):
    j = np.arange(0, D, 2, dtype=np.float32)
    inv = (1.0 / (10000.0 ** (j / np.float32(D)))).astype(np.float32)
    t = positions.astype(np.float32)
    freqs = (inv[:, None] * t[None, :]).astype(np.float32)
    c = np.cos(freqs.astype(np.float64)).astype(np.float32)
    s = np.sin(freqs.astype(np.float64)).astype(np.float32)
    tabC = np.ascontiguousarray(np.concatenate([c, c], axis=0))
    tabS = np.ascontiguousarray(np.concatenate([s, -s], axis=0))
    return tabC, tabS


def _tile_lhsT(wT, nblk):
    K_in = wT.shape[0]
    a = np.asarray(wT).reshape(K_in // 128, 128, nblk, 128)
    return np.ascontiguousarray(a.transpose(2, 1, 0, 3))


def host_prep(inp):
    perm = _perm_head()
    perm_q = np.concatenate([h * D + perm for h in range(NQ)])
    perm_k = np.concatenate([h * D + perm for h in range(NKV)])

    x = np.asarray(inp["L0_attn_x"], dtype=np.float32)
    WqT = np.ascontiguousarray(np.asarray(inp["L0_attn_Wq"], dtype=np.float32).T[:, perm_q])
    WkT = np.ascontiguousarray(np.asarray(inp["L0_attn_Wk"], dtype=np.float32).T[:, perm_k])
    WvT = np.ascontiguousarray(np.asarray(inp["L0_attn_Wv"], dtype=np.float32).T)
    WoT = np.ascontiguousarray(np.asarray(inp["L0_attn_Wo"], dtype=np.float32).T)

    shared = {
        "wq": _tile_lhsT(WqT, NCH),
        "wk": _tile_lhsT(WkT, NKV),
        "wv": np.ascontiguousarray(WvT.reshape(NCH, 128, NKV * 128).transpose(1, 0, 2)),
        "wo": _tile_lhsT(WoT, NCH),
        "la4": np.ascontiguousarray(np.stack(
            [np.asarray(inp[k], dtype=np.float32).T.reshape(NCH, 128, RANK).transpose(1, 0, 2).reshape(128, NCH * RANK)
             for k in ("L0_attn_qA", "L0_attn_kA", "L0_attn_vA", "L0_attn_oA")], axis=1)),
        "qb": np.ascontiguousarray(np.asarray(inp["L0_attn_qB"], dtype=np.float32).T[:, perm_q]),
        "kb": np.ascontiguousarray(np.asarray(inp["L0_attn_kB"], dtype=np.float32).T[:, perm_k]),
        "vb": np.ascontiguousarray(np.asarray(inp["L0_attn_vB"], dtype=np.float32).T),
        "ob": np.ascontiguousarray(np.asarray(inp["L0_attn_oB"], dtype=np.float32).T),
        "wg": _tile_lhsT(np.asarray(inp["L0_ffn_w_gate"]), NIB),
        "wu": _tile_lhsT(np.asarray(inp["L0_ffn_w_up"]), NIB),
        "wd": _tile_lhsT(np.asarray(inp["L0_ffn_w_down"]), NCH),
        "sg": np.ascontiguousarray(np.asarray(inp["L0_ffn_s_gate"], dtype=np.float32).reshape(NIB, 128).T),
        "su": np.ascontiguousarray(np.asarray(inp["L0_ffn_s_up"], dtype=np.float32).reshape(NIB, 128).T),
        "sd": np.ascontiguousarray(np.asarray(inp["L0_ffn_s_down"], dtype=np.float32).reshape(NCH, 128).T),
    }
    kc, ks = _rope_tables(np.arange(S))
    shared["ropekc"] = kc
    shared["ropeks"] = ks

    xbt = []
    for b in range(B):
        xbt.append(np.ascontiguousarray(x[b].T.reshape(NCH, 128, S)))

    in_maps, own_all = [], []
    for c in range(8):
        b, r = divmod(c, 4)
        blocks = [4 * (e - 1) + r for e in range(SLOTS, 0, -1)]
        own = np.concatenate([np.arange(j * 128, (j + 1) * 128) for j in blocks])
        own_all.append((b, own))
        xo = np.ascontiguousarray(x[b][own].T)
        qc, qs = _rope_tables(own)
        m = np.zeros((128, SLOTS, 4, 128), dtype=BF2)
        kk = np.arange(128)[:, None]
        qq = np.arange(128)[None, :]
        for sp in range(SLOTS):
            for jj in range(4):
                if jj < r:
                    m[:, sp, jj, :] = 1.0
                elif jj == r:
                    m[:, sp, jj, :] = (kk <= qq).astype(BF2)
        in_maps.append(dict(
            shared,
            xbt=xbt[b],
            xot=np.ascontiguousarray(xo.reshape(NCH, 128, T)),
            ropeqc=qc, ropeqs=qs,
            masks=np.ascontiguousarray(m.reshape(128, SLOTS * 4 * 128)),
        ))
    return in_maps, own_all


def build_program(taps=()):
    nc = bass.Bass("TRN2", target_bir_lowering=False)

    I = {}
    def din(name, shape, dtype):
        I[name] = nc.dram_tensor(name, list(shape), dtype, kind="ExternalInput").ap()

    din("xbt", [NCH, 128, S], F32)
    din("xot", [NCH, 128, T], F32)
    din("wq", [NCH, 128, NCH, 128], F32)
    din("wk", [NKV, 128, NCH, 128], F32)
    din("wv", [128, NCH, NKV * 128], F32)
    din("wo", [NCH, 128, NCH, 128], F32)
    din("la4", [128, 4, NCH * RANK], F32)
    din("qb", [RANK, NQ * 128], F32)
    din("kb", [RANK, NKV * 128], F32)
    din("vb", [RANK, NKV * 128], F32)
    din("ob", [RANK, NCH * 128], F32)
    din("wg", [NIB, 128, NCH, 128], I8)
    din("wu", [NIB, 128, NCH, 128], I8)
    din("wd", [NCH, 128, NIB, 128], I8)
    din("sg", [128, NIB], F32)
    din("su", [128, NIB], F32)
    din("sd", [128, NCH], F32)
    din("ropekc", [128, S], F32)
    din("ropeks", [128, S], F32)
    din("ropeqc", [128, T], F32)
    din("ropeqs", [128, T], F32)
    din("masks", [128, SLOTS * 4 * 128], BF16)

    yout = nc.dram_tensor("yout", [NCH, 128, T], F32, kind="ExternalOutput").ap()
    tk = lambda n: "ExternalOutput" if n in taps else "Internal"
    kspill = nc.dram_tensor("kspill", [NKV, 128, S], BF16, kind=tk("kspill")).ap()
    vspill = nc.dram_tensor("vspill", [2, 128, KB * 512], BF16, kind=tk("vspill")).ap()
    tapd = {}
    for nm in taps:
        if nm in ("kspill", "vspill"):
            continue
        if nm == "dbg":
            tapd[nm] = nc.dram_tensor("tap_dbg", [128, 64], F32, kind="ExternalOutput").ap()
            continue
        dt = BF16 if nm in ("qt", "ot") else F32
        tapd[nm] = nc.dram_tensor("tap_" + nm, [128, NCH * 1024], dt, kind="ExternalOutput").ap()

    AR = Arena(nc, 206 * 1024)
    DOF = 163840
    s32 = AR.tensor("s32", 0, [128, NCH * 1024], F32)
    s16 = AR.tensor("s16", 65536, [128, NCH * 1024], BF16)
    big = AR.tensor("big", 98304, [128, 32768], BF16)
    xcb = AR.tensor("xcb", 32768, [128, 2, NCH * 512], BF16)
    kvstage = AR.tensor("kvstage", 0, [128, 4, 512], BF16)
    xotb = AR.tensor("xotb", 98304, [128, NCH * 1024], BF16)

    def At(name, off, shape, dtype):
        return AR.tensor(name, DOF + off, list(shape), dtype)

    gpanb = At("gpanb", 0, [128, 2, 2048], BF16)
    wpanb = At("wpanb", 8192, [128, 2, 2048], BF16)
    ropec = At("ropec", 16384, [128, 1024], F32)
    ropes = At("ropes", 20480, [128, 1024], F32)
    qscr = At("qscr", 24576, [128, 1024], F32)
    qscr2 = At("qscr2", 28672, [128, 1024], F32)
    sqa = At("sqa", 32768, [RANK, 1024], BF16)
    sqsq = At("sqsq", 34816, [128, 1024], BF16)
    maskv = At("maskv", 16384, [128, SLOTS * 4 * 128], BF16)
    ptile = At("ptile", 24576, [128, 2, 1024], BF16)
    sbc = At("sbc", 28672, [128, 1024], F32)
    mubc = At("mubc", 16384, [128, 1024], F32)
    rstdbc = At("rstdbc", 20480, [128, 1024], F32)
    rowsA = At("rowsA", 24576, [1, 1024], F32)
    rowsB = At("rowsB", 28672, [1, 1024], F32)
    rowsC = At("rowsC", 32768, [1, 1024], F32)
    wdb = At("wdb", 16384, [128, 2, 4096], BF16)
    gbuf = At("gbuf", 32768, [128, 2, 512], BF16)
    upanb = At("upanb", 34816, [128, 2, 2048], BF16)
    lorab = At("lorab", 38912, [RANK, 2048], BF16)
    la4b = At("la4b", 43008, [128, 4, 128], BF16)
    onesb = At("onesb", 44032, [128, 1], BF16)
    ones32 = At("ones32", 44064, [1, 128], F32)
    negc = At("negc", 44576, [128, 16], F32)
    kmaxsq = At("kmaxsq", 44640, [1, 16], F32)
    qmaxsq = At("qmaxsq", 44704, [1, 8], F32)
    qmaxs2 = At("qmaxs2", 44736, [1, 8], F32)
    scrow = At("scrow", 44768, [1, 8], F32)
    scrow2 = At("scrow2", 44800, [1, 8], F32)
    sgc = At("sgc", 44832, [128, NIB], F32)
    suc = At("suc", 45088, [128, NIB], F32)
    sdc = At("sdc", 45344, [128, NCH], F32)
    cst = At("cst", 45408, [128, 4], F32)

    K = AR.keys

    p2a = nc.alloc_psum_tensor("p2a", [128, 1024], F32)
    p2b = nc.alloc_psum_tensor("p2b", [128, 1024], F32)
    p2c = nc.alloc_psum_tensor("p2c", [128, 1024], F32)
    prow = nc.alloc_psum_tensor("prow", [1, 1024], F32)
    PKEY = {"p2a": [("P", 0), ("P", 1)], "p2b": [("P", 2), ("P", 3)],
            "p2c": [("P", 4), ("P", 5)], "prow": [("P", 6), ("P", 7)]}

    def pkh(name, lo_el, hi_el):
        ks = PKEY[name]
        out = []
        if lo_el < 512:
            out.append(ks[0])
        if hi_el > 512:
            out.append(ks[1])
        return out

    sch = Sched()

    def MM(out_ap, lhsT_ap, rhs_ap, start, stop, reads, writes):
        sch.add("pe", lambda e, o=out_ap, l=lhsT_ap, r=rhs_ap, s1=start, s2=stop:
                e.matmul(o, l, r, start=s1, stop=s2), reads=reads, writes=writes)

    def DMA(eng, out_ap, in_ap, reads, writes, stream):
        return sch.add(eng, lambda e, o=out_ap, i=in_ap: e.dma_start(out=o, in_=i),
                       reads=reads, writes=writes, dma=True, stream=stream)

    def CAST(eng, out_ap, in_ap, reads, writes):
        if eng == "ac":
            sch.add(eng, lambda e, o=out_ap, i=in_ap: e.copy(out=o, in_=i),
                    reads=reads, writes=writes)
        else:
            sch.add(eng, lambda e, o=out_ap, i=in_ap: e.tensor_copy(out=o, in_=i),
                    reads=reads, writes=writes)

    def TT(eng, op, out_ap, a_ap, b_ap, reads, writes):
        sch.add(eng, lambda e, o=out_ap, a=a_ap, b=b_ap, al=op:
                e.tensor_tensor(out=o, in0=a, in1=b, op=al), reads=reads, writes=writes)

    def panel_ap(inp, j):
        return I[inp][j:j + 1].rearrange("a p c q -> p (a c) q").rearrange("p c q -> p (c q)")

    sch.add("dv", lambda e: e.memset(onesb[:], 1.0), writes=K("onesb"))
    sch.add("dv", lambda e: e.memset(ones32[:], 1.0), writes=K("ones32"))
    sch.add("dv", lambda e: e.memset(kmaxsq[:], 0.0), writes=K("kmaxsq"))
    sch.add("dv", lambda e: e.memset(cst[:, 0:1], EPS), writes=K("cst"))
    sch.add("dv", lambda e: e.memset(cst[:, 1:2], 1.0 / D), writes=K("cst"))
    sch.add("dv", lambda e: e.memset(cst[:, 2:3], SCALE), writes=K("cst"))
    DMA("sy", sgc[:], I["sg"], [], K("sgc"), "scg")
    DMA("sy", suc[:], I["su"], [], K("suc"), "scu")
    DMA("sy", sdc[:], I["sd"], [], K("sdc"), "scd")
    DMA("sy", ropec[:], I["ropeqc"], [], K("ropec"), "ropec")
    DMA("sy", ropes[:], I["ropeqs"], [], K("ropes"), "ropes")
    DMA("gp", xotb[:].rearrange("p (c t) -> p c t", t=1024), I["xot"].rearrange("c p t -> p c t"),
        K("xotb"), K("xotb"), "xotb")
    DMA("gp", wpanb[:, 0, :], panel_ap("wq", 0), K("wpanb", 0, 2048), K("wpanb", 0, 2048), "panq0")
    DMA("gp", wpanb[:, 1, :], panel_ap("wq", 1), K("wpanb", 2048, 4096), K("wpanb", 2048, 4096), "panq1")
    DMA("gp", la4b[:], I["la4"], K("la4b"), K("la4b"), "laA")
    laab = la4b[:, 0, :]
    laabK = la4b[:, 1, :]
    laab2 = la4b[:, 2, :]
    laabO = la4b[:, 3, :]
    DMA("gp", lorab[:], I["qb"], K("lorab"), K("lorab"), "lqb")

    pend_qnorms = []
    for bq in range(NQ):
        par = bq % 2
        wb = wpanb[:, par, :]
        wbk = K("wpanb", par * 2048, (par + 1) * 2048)
        pq, pqn = (p2a, "p2a") if par == 0 else (p2b, "p2b")
        for sH in range(2):
            sl = slice(sH * 512, (sH + 1) * 512)
            for c in range(NCH):
                MM(pq[:, sl], wb[:, c * 128:(c + 1) * 128],
                   xotb[:, c * 1024 + sH * 512: c * 1024 + (sH + 1) * 512],
                   c == 0, False,
                   wbk + K("xotb", c * 1024 + sH * 512, c * 1024 + (sH + 1) * 512),
                   pkh(pqn, sH * 512, (sH + 1) * 512))
        if bq == 0:
            for sH2 in range(2):
                for c in range(NCH):
                    MM(p2b[0:RANK, sH2 * 512:(sH2 + 1) * 512], laab[:, c * RANK:(c + 1) * RANK],
                       xotb[:, c * 1024 + sH2 * 512: c * 1024 + (sH2 + 1) * 512],
                       c == 0, c == NCH - 1,
                       K("la4b") + K("xotb", c * 1024 + sH2 * 512, c * 1024 + (sH2 + 1) * 512),
                       pkh("p2b", sH2 * 512, (sH2 + 1) * 512))
            sch.add("dv", lambda e: e.tensor_scalar_mul(out=sqa[:], in0=p2b[0:RANK, :], scalar1=2.0),
                    reads=PKEY["p2b"], writes=K("sqa"))
        for sH in range(2):
            sl = slice(sH * 512, (sH + 1) * 512)
            MM(pq[:, sl], lorab[0:RANK, bq * 128:(bq + 1) * 128], sqa[0:RANK, sl],
               False, True, K("lorab") + K("sqa"), pkh(pqn, sH * 512, (sH + 1) * 512))
        if bq + 2 < NQ:
            DMA("gp", wpanb[:, par, :], panel_ap("wq", bq + 2), wbk, wbk, "panq%d" % par)
        TT("dv", ALU.mult, qscr[:], pq[:, 0:1024], ropec[:], PKEY[pqn] + K("ropec"), K("qscr"))
        TT("dv", ALU.mult, qscr2[0:64, :], pq[64:128, 0:1024], ropes[64:128, :],
           PKEY[pqn] + K("ropes"), K("qscr2"))
        TT("dv", ALU.mult, qscr2[64:128, :], pq[0:64, 0:1024], ropes[0:64, :],
           PKEY[pqn] + K("ropes"), K("qscr2"))
        TT("dv", ALU.add, s16[:, bq * 1024:(bq + 1) * 1024], qscr[:], qscr2[:],
           K("qscr") + K("qscr2"), K("s16", bq * 1024, (bq + 1) * 1024))
        if len(pend_qnorms) >= 2:
            pend_qnorms.pop(0)()
        def qnorm(bq=bq):
            prwk = PKEY["prow"] if bq % 2 == 0 else PKEY["p2c"]
            sch.add("ac", lambda e, i=s16[:, bq * 1024:(bq + 1) * 1024], o=sqsq[:]:
                    e.activation(out=o, in_=i, func=AF.Square),
                    reads=K("s16", bq * 1024, (bq + 1) * 1024), writes=K("sqsq"))
            for sH in range(2):
                out = prow[0:1, sH * 512:(sH + 1) * 512] if bq % 2 == 0 else p2c[0:1, sH * 512:(sH + 1) * 512]
                MM(out, onesb[:], sqsq[:, sH * 512:(sH + 1) * 512],
                   True, True, K("onesb") + K("sqsq"), [prwk[sH]])
            qdst = qmaxsq if bq % 2 == 0 else qmaxs2
            src = prow[0:1, 0:1024] if bq % 2 == 0 else p2c[0:1, 0:1024]
            sch.add("dv", lambda e, o=qdst[0:1, bq // 2:bq // 2 + 1], i=src:
                    e.reduce_max(out=o, in_=i, axis=AXX),
                    reads=prwk, writes=K("qmaxsq") + K("qmaxs2"))
        pend_qnorms.append(qnorm)
    while pend_qnorms:
        pend_qnorms.pop(0)()
    if "qt" in tapd:
        DMA("gp", tapd["qt"], s16[:], K("s16"), [], "tapqt")

    DMA("gp", lorab[0:RANK, 0:1024], I["kb"], K("lorab", 0, 1024), K("lorab", 0, 1024), "lkb")
    DMA("gp", lorab[0:RANK, 1024:2048], I["vb"], K("lorab", 1024, 2048), K("lorab", 1024, 2048), "lvb")
    for kh in range(NKV):
        DMA("gp", big[:, kh * 2048:(kh + 1) * 2048], panel_ap("wk", kh),
            K("big", kh * 2048, (kh + 1) * 2048), K("big", kh * 2048, (kh + 1) * 2048), "wk%d" % kh)
    for cp in range(NCH // 2):
        DMA("gp", big[:, 16384 + cp * 2048:16384 + (cp + 1) * 2048].rearrange("p (a t) -> p a t", a=2),
            I["wv"][:, 2 * cp:2 * cp + 2, :],
            K("big", 16384 + cp * 2048, 16384 + (cp + 1) * 2048),
            K("big", 16384 + cp * 2048, 16384 + (cp + 1) * 2048), "wv%d" % cp)
    wkb = big[:, 0:16384]
    wvb = big[:, 16384:32768]

    pend_knorms = []
    for tc in range(TC):
        tsl = slice(tc * 512, (tc + 1) * 512)
        par = tc % 2
        xv = xcb[:, par, :]
        xvk = K("xcb", par * NCH * 512, (par + 1) * NCH * 512)
        DMA("gp", xv.rearrange("p (c t) -> p c t", t=512),
            I["xbt"][:, :, tsl].rearrange("c p t -> p c t"), xvk, xvk, "xcb%d" % par)
        DMA("sy", ropec[:, 0:512], I["ropekc"][:, tsl], K("ropec"), K("ropec", 0, 512), "ropec")
        DMA("sy", ropes[:, 0:512], I["ropeks"][:, tsl], K("ropes"), K("ropes", 0, 512), "ropes")
        for c in range(NCH):
            MM(p2b[0:RANK, 0:512], laabK[:, c * RANK:(c + 1) * RANK], xv[:, c * 512:(c + 1) * 512],
               c == 0, c == NCH - 1, K("la4b") + xvk, pkh("p2b", 0, 512))
        sch.add("dv", lambda e: e.tensor_scalar_mul(out=sqa[0:RANK, 0:512], in0=p2b[0:RANK, 0:512], scalar1=2.0),
                reads=pkh("p2b", 0, 512), writes=K("sqa", 0, 512))
        for c in range(NCH):
            MM(p2b[0:RANK, 512:1024], laab2[:, c * RANK:(c + 1) * RANK], xv[:, c * 512:(c + 1) * 512],
               c == 0, c == NCH - 1, K("la4b") + xvk, pkh("p2b", 512, 1024))
        sch.add("dv", lambda e: e.tensor_scalar_mul(out=sqa[0:RANK, 512:1024], in0=p2b[0:RANK, 512:1024], scalar1=2.0),
                reads=pkh("p2b", 512, 1024), writes=K("sqa", 512, 1024))
        ska, skak = sqa[0:RANK, 0:512], K("sqa", 0, 512)
        sva, svak = sqa[0:RANK, 512:1024], K("sqa", 512, 1024)
        for kh in range(NKV):
            kp = kh % 2
            pk = p2a[:, kp * 512:(kp + 1) * 512]
            pkk = pkh("p2a", kp * 512, (kp + 1) * 512)
            for c in range(NCH):
                MM(pk, wkb[:, kh * 2048 + c * 128: kh * 2048 + (c + 1) * 128],
                   xv[:, c * 512:(c + 1) * 512], c == 0, False,
                   K("big", kh * 2048, (kh + 1) * 2048) + xvk, pkk)
            MM(pk, lorab[0:RANK, kh * 128:(kh + 1) * 128], ska, False, True,
               K("lorab", 0, 1024) + skak, pkk)
            if len(pend_knorms) >= 2:
                pend_knorms.pop(0)()
            kst = kvstage[:, kh % 4, :]
            kstk = K("kvstage", (kh % 4) * 512, (kh % 4 + 1) * 512)
            TT("dv", ALU.mult, qscr[:, 0:512], pk, ropec[:, 0:512], pkk + K("ropec", 0, 512), K("qscr", 0, 512))
            TT("dv", ALU.mult, qscr2[0:64, 0:512], pk[64:128, :], ropes[64:128, 0:512],
               pkk + K("ropes", 0, 512), K("qscr2", 0, 512))
            TT("dv", ALU.mult, qscr2[64:128, 0:512], pk[0:64, :], ropes[0:64, 0:512],
               pkk + K("ropes", 0, 512), K("qscr2", 0, 512))
            TT("dv", ALU.add, kst, qscr[:, 0:512], qscr2[:, 0:512],
               K("qscr", 0, 512) + K("qscr2", 0, 512), kstk)
            DMA("sy", kspill[kh:kh + 1, :, tsl].rearrange("a p t -> p (a t)"),
                kst, kstk, [("KS", kh, tc)], "ks")
            def knorm(kh=kh, kst=kst, kstk=kstk):
                kq = (kh % 2) * 512
                sch.add("ac", lambda e: e.activation(out=sqsq[:, kq:kq + 512], in_=kst, func=AF.Square),
                        reads=kstk, writes=K("sqsq", kq, kq + 512))
                MM(prow[0:1, kq:kq + 512], onesb[:], sqsq[:, kq:kq + 512], True, True,
                   K("onesb") + K("sqsq", kq, kq + 512), pkh("prow", kq, kq + 512))
                sch.add("dv", lambda e, o=scrow2[0:1, 0:1] if kh % 2 else scrow[0:1, 0:1]:
                        e.reduce_max(out=o, in_=prow[0:1, kq:kq + 512], axis=AXX),
                        reads=pkh("prow", kq, kq + 512), writes=K("scrow") + K("scrow2"))
                sch.add("dv", lambda e, o=kmaxsq[0:1, kh:kh + 1],
                        i=scrow2[0:1, 0:1] if kh % 2 else scrow[0:1, 0:1]:
                        e.tensor_tensor(out=o, in0=i, in1=o, op=ALU.max),
                        reads=K("scrow") + K("scrow2") + K("kmaxsq"), writes=K("kmaxsq"))
            pend_knorms.append(knorm)
        while pend_knorms:
            pend_knorms.pop(0)()
        for tb in range(4):
            for vq in range(2):
                vp = (tb * 2 + vq) % 2
                pv = p2c[:, vp * 512:(vp + 1) * 512]
                pvk = pkh("p2c", vp * 512, (vp + 1) * 512)
                for c in range(NCH):
                    MM(pv, xv[:, c * 512 + tb * 128: c * 512 + (tb + 1) * 128],
                       wvb[:, c * 1024 + vq * 512: c * 1024 + (vq + 1) * 512],
                       c == 0, False,
                       xvk + K("big", 16384 + c * 1024 + vq * 512, 16384 + c * 1024 + (vq + 1) * 512), pvk)
                MM(pv, sva[0:RANK, tb * 128:(tb + 1) * 128],
                   lorab[0:RANK, 1024 + vq * 512: 1024 + (vq + 1) * 512],
                   False, True, svak + K("lorab", 1024, 2048), pvk)
                CAST("dv", sqsq[:, 0:512], pv, pvk, K("sqsq", 0, 512))
                DMA("sy", vspill[vq:vq + 1, :, (tc * 4 + tb) * 512:(tc * 4 + tb + 1) * 512]
                    .rearrange("a p t -> p (a t)"),
                    sqsq[:, 0:512], K("sqsq", 0, 512), [("VS", vq, tc * 4 + tb)], "vs")

    for qsrc, sdst in ((qmaxsq, scrow), (qmaxs2, scrow2)):
        sch.add("dv", lambda e, a=qsrc, o=sdst: e.tensor_tensor(
                out=o[0:1, 0:8], in0=a[0:1, 0:8], in1=kmaxsq[0:1, 0:8], op=ALU.mult),
                reads=K("qmaxsq") + K("qmaxs2") + K("kmaxsq"), writes=K("scrow") + K("scrow2"))
        sch.add("ac", lambda e, o=sdst: e.activation(out=o[0:1, 0:8], in_=o[0:1, 0:8],
                                                     func=AF.Sqrt, scale=cst[0:1, 1:2]),
                reads=K("scrow") + K("scrow2") + K("cst"), writes=K("scrow") + K("scrow2"))
        sch.add("dv", lambda e, o=sdst: e.tensor_scalar_mul(out=o[0:1, 0:8], in0=o[0:1, 0:8],
                                                            scalar1=-1.0),
                reads=K("scrow") + K("scrow2"), writes=K("scrow") + K("scrow2"))
    MM(p2c[:, 0:8], ones32[:], scrow[0:1, 0:8], True, True,
       K("ones32") + K("scrow"), pkh("p2c", 0, 8))
    MM(p2c[:, 8:16], ones32[:], scrow2[0:1, 0:8], True, True,
       K("ones32") + K("scrow2"), pkh("p2c", 0, 16))
    sch.add("dv", lambda e: e.tensor_copy(out=negc[:], in_=p2c[:, 0:16]),
            reads=pkh("p2c", 0, 16), writes=K("negc"))
    if "dbg" in tapd:
        DMA("gp", tapd["dbg"][:, 0:16], negc[:], K("negc"), [], "tapd0")
        DMA("gp", tapd["dbg"][0:1, 16:32], kmaxsq[:], K("kmaxsq"), [], "tapd1")
        DMA("gp", tapd["dbg"][0:1, 32:40], qmaxsq[:], K("qmaxsq"), [], "tapd2")
        DMA("gp", tapd["dbg"][0:1, 40:48], qmaxs2[:], K("qmaxs2"), [], "tapd3")

    DMA("sy", maskv[:], I["masks"], K("maskv"), K("maskv"), "mask")
    DMA("sy", s32[:].rearrange("p (c t) -> p c t", t=T), I["xot"].rearrange("c p t -> p c t"),
        K("s32"), K("s32"), "xot")
    DMA("gp", lorab[0:RANK, :], I["ob"], K("lorab"), K("lorab"), "lob")

    OT = big[:, 0:16384]
    stv = [0]

    def st_step(kh, h, kb):
        kp = kh % 2
        kbuf = big[:, 16384 + kp * 4096: 16384 + (kp + 1) * 4096]
        kbk = K("big", 16384 + kp * 4096, 16384 + (kp + 1) * 4096)
        n = 8 - kb // 4
        nw = n * 128
        stp = stv[0]
        pst, pstn = (p2a, "p2a") if stp == 0 else (p2b, "p2b")
        for s0 in range(0, nw, 512):
            s1 = min(s0 + 512, nw)
            MM(pst[:, s0:s1], kbuf[:, kb * 128:(kb + 1) * 128],
               s16[:, h * 1024 + s0: h * 1024 + s1], True, True,
               kbk + K("s16", h * 1024 + s0, h * 1024 + s1), pkh(pstn, s0, s1))
        stv[0] ^= 1
        return pst, pstn, n, nw

    def fin_step(kh, h, kb, pst, pstn, n, nw):
        kp = kh % 2
        vbuf = big[:, 24576 + kp * 4096: 24576 + (kp + 1) * 4096]
        vbk = K("big", 24576 + kp * 4096, 24576 + (kp + 1) * 4096)
        pp = kb % 2
        ptv = ptile[:, pp, :]
        qp = (h % 2) * 8 + h // 2
        rd = pkh(pstn, 0, nw) + K("negc") + K("cst")
        sch.add("ac", lambda e, o=ptv[:, 0:nw], i=pst[:, 0:nw], bi=negc[:, qp:qp + 1]:
                e.activation(out=o, in_=i, func=AF.Exp, bias=bi, scale=cst[:, 2:3]),
                reads=rd, writes=K("ptile", pp * 1024, pp * 1024 + nw))
        sp, jj = kb // 4, kb % 4
        mcol = (sp * 4 + jj) * 128
        TT("dv", ALU.mult, ptv[:, nw - 128:nw], ptv[:, nw - 128:nw],
           maskv[:, mcol:mcol + 128],
           K("ptile", pp * 1024 + nw - 128, pp * 1024 + nw) + K("maskv", mcol, mcol + 128),
           K("ptile", pp * 1024 + nw - 128, pp * 1024 + nw))
        for s0 in range(0, nw, 512):
            s1 = min(s0 + 512, nw)
            MM(prow[0:1, s0:s1], onesb[:], ptv[:, s0:s1],
               kb == 0, (kb == KB - 1) if s0 == 0 else (kb == 15),
               K("onesb") + K("ptile", pp * 1024 + s0, pp * 1024 + s1),
               pkh("prow", s0, s1))
            MM(p2c[:, s0:s1], vbuf[:, kb * 128:(kb + 1) * 128], ptv[:, s0:s1],
               kb == 0, (kb == KB - 1) if s0 == 0 else (kb == 15),
               vbk + K("ptile", pp * 1024 + s0, pp * 1024 + s1),
               pkh("p2c", s0, s1))

    def head_tail(h):
        sch.add("ac", lambda e: e.activation(out=sbc[0:1, 0:1024], in_=prow[0:1, 0:1024],
                                             func=AF.Ln),
                reads=PKEY["prow"], writes=K("sbc", 0, 1024))
        sch.add("ac", lambda e: e.activation(out=sbc[0:1, 0:1024], in_=sbc[0:1, 0:1024],
                                             func=AF.Exp, scale=-1.0),
                reads=K("sbc", 0, 1024), writes=K("sbc", 0, 1024))
        stp = stv[0]
        bps, bkn = (p2a, "p2a") if stp == 0 else (p2b, "p2b")
        for s0 in (0, 512):
            MM(bps[:, s0:s0 + 512], ones32[:], sbc[0:1, s0:s0 + 512], True, True,
               K("ones32") + K("sbc", 0, 1024), pkh(bkn, s0, s0 + 512))
        sch.add("ac", lambda e, p=bps: e.copy(out=sbc[:], in_=p[:, 0:1024]),
                reads=PKEY[bkn], writes=K("sbc"))
        sch.add("dv", lambda e, o=OT[:, h * 1024:(h + 1) * 1024]:
                e.tensor_tensor(out=o, in0=p2c[:, 0:1024], in1=sbc[:], op=ALU.mult),
                reads=PKEY["p2c"] + K("sbc"), writes=K("big", h * 1024, (h + 1) * 1024))

    steps = []
    for kh in range(NKV):
        for h in (2 * kh, 2 * kh + 1):
            for kb in range(KB):
                steps.append((kh, h, kb))
    pend = []
    for kh in range(NKV):
        kp = kh % 2
        kbuf = big[:, 16384 + kp * 4096: 16384 + (kp + 1) * 4096]
        kbk = K("big", 16384 + kp * 4096, 16384 + (kp + 1) * 4096)
        vbuf = big[:, 24576 + kp * 4096: 24576 + (kp + 1) * 4096]
        vbk = K("big", 24576 + kp * 4096, 24576 + (kp + 1) * 4096)
        DMA("gp", kbuf, kspill[kh:kh + 1].rearrange("a p t -> p (a t)"),
            [("KS", kh, t_) for t_ in range(TC)] + kbk, kbk, "kbuf%d" % kp)
        vq, w = kh // 4, kh % 4
        vsrc = (vspill[vq:vq + 1].rearrange("a p t -> p (a t)")
                .rearrange("p (tb d) -> p tb d", d=512)[:, :, w * 128:(w + 1) * 128])
        DMA("gp", vbuf.rearrange("p (tb d) -> p tb d", d=128), vsrc,
            [("VS", vq, t_) for t_ in range(KB)] + vbk, vbk, "vbuf%d" % kp)
        for h in (2 * kh, 2 * kh + 1):
            for kb in range(KB):
                ctx = st_step(kh, h, kb)
                if pend:
                    fn = pend.pop(0)
                    fn()
                if kb < KB - 1:
                    pend.append(lambda kh=kh, h=h, kb=kb, c=ctx: fin_step(kh, h, kb, *c))
                else:
                    def last(kh=kh, h=h, kb=kb, c=ctx):
                        fin_step(kh, h, kb, *c)
                        head_tail(h)
                    pend.append(last)
    for fn in pend:
        fn()
    if "ot" in tapd:
        DMA("gp", tapd["ot"], OT, K("big", 0, 16384), [], "tapot")

    for sH in range(2):
        for c in range(NCH):
            MM(p2b[0:RANK, sH * 512:(sH + 1) * 512], laabO[:, c * RANK:(c + 1) * RANK],
               OT[:, c * 1024 + sH * 512: c * 1024 + (sH + 1) * 512],
               c == 0, c == NCH - 1,
               K("la4b") + K("big", c * 1024 + sH * 512, c * 1024 + (sH + 1) * 512),
               pkh("p2b", sH * 512, (sH + 1) * 512))
    sch.add("dv", lambda e: e.tensor_scalar_mul(out=sqa[:], in0=p2b[0:RANK, :], scalar1=2.0),
            reads=PKEY["p2b"], writes=K("sqa"))
    DMA("gp", wpanb[:, 0, :], panel_ap("wo", 0), K("wpanb", 0, 2048), K("wpanb", 0, 2048), "pano0")
    DMA("gp", wpanb[:, 1, :], panel_ap("wo", 1), K("wpanb", 2048, 4096), K("wpanb", 2048, 4096), "pano1")
    for ob in range(NCH):
        par = ob % 2
        wb = wpanb[:, par, :]
        wbk = K("wpanb", par * 2048, (par + 1) * 2048)
        pw, pwn = (p2a, "p2a") if par == 0 else (p2b, "p2b")
        for sH in range(2):
            sl = slice(sH * 512, (sH + 1) * 512)
            for c in range(NCH):
                MM(pw[:, sl], wb[:, c * 128:(c + 1) * 128],
                   OT[:, c * 1024 + sH * 512: c * 1024 + (sH + 1) * 512], c == 0, False,
                   wbk + K("big", c * 1024 + sH * 512, c * 1024 + (sH + 1) * 512),
                   pkh(pwn, sH * 512, (sH + 1) * 512))
            MM(pw[:, sl], lorab[0:RANK, ob * 128:(ob + 1) * 128], sqa[0:RANK, sl],
               False, True, K("lorab") + K("sqa"), pkh(pwn, sH * 512, (sH + 1) * 512))
        if ob + 2 < NCH:
            DMA("gp", wpanb[:, par, :], panel_ap("wo", ob + 2), wbk, wbk, "pano%d" % par)
        sch.add("dv", lambda e, o=s32[:, ob * 1024:(ob + 1) * 1024], p=pw:
                e.tensor_tensor(out=o, in0=p[:, 0:1024], in1=o, op=ALU.add),
                reads=PKEY[pwn] + K("s32", ob * 1024, (ob + 1) * 1024),
                writes=K("s32", ob * 1024, (ob + 1) * 1024))
    if "r1" in tapd:
        DMA("gp", tapd["r1"], s32[:], K("s32"), [], "tapr1")

    def layernorm(emit_bf16):
        for c in range(NCH):
            csl = slice(c * 1024, (c + 1) * 1024)
            ck = K("s16", c * 1024, (c + 1) * 1024)
            CAST("dv", s16[:, csl], s32[:, csl], K("s32", c * 1024, (c + 1) * 1024), ck)
            sch.add("ac", lambda e, i=s16[:, csl], o=big[:, csl]:
                    e.activation(out=o, in_=i, func=AF.Square),
                    reads=ck, writes=K("big", c * 1024, (c + 1) * 1024))
            for sH in range(2):
                MM(prow[0:1, sH * 512:(sH + 1) * 512], onesb[:],
                   s16[:, c * 1024 + sH * 512: c * 1024 + (sH + 1) * 512],
                   c == 0, c == NCH - 1,
                   K("onesb") + ck, [PKEY["prow"][sH]])
                MM(p2c[0:1, sH * 512:(sH + 1) * 512], onesb[:],
                   big[:, c * 1024 + sH * 512: c * 1024 + (sH + 1) * 512],
                   c == 0, c == NCH - 1,
                   K("onesb") + K("big", c * 1024 + sH * 512, c * 1024 + (sH + 1) * 512),
                   [PKEY["p2c"][sH]])
        sch.add("dv", lambda e: e.tensor_scalar_mul(out=rowsA[:], in0=prow[0:1, :], scalar1=1.0 / H),
                reads=PKEY["prow"], writes=K("rowsA"))
        sch.add("dv", lambda e: e.tensor_scalar_mul(out=rowsB[:], in0=p2c[0:1, :], scalar1=1.0 / H),
                reads=PKEY["p2c"], writes=K("rowsB"))
        TT("dv", ALU.mult, rowsC[:], rowsA[:], rowsA[:], K("rowsA"), K("rowsC"))
        TT("dv", ALU.subtract, rowsB[:], rowsB[:], rowsC[:], K("rowsB") + K("rowsC"), K("rowsB"))
        sch.add("ac", lambda e: e.activation(out=rowsB[:], in_=rowsB[:], func=AF.Sqrt,
                                             bias=cst[0:1, 0:1]),
                reads=K("rowsB") + K("cst"), writes=K("rowsB"))
        sch.add("ac", lambda e: e.activation(out=rowsC[:], in_=rowsB[:], func=AF.Ln),
                reads=K("rowsB"), writes=K("rowsC"))
        sch.add("ac", lambda e: e.activation(out=rowsC[:], in_=rowsC[:], func=AF.Exp, scale=-1.0),
                reads=K("rowsC"), writes=K("rowsC"))
        for s0 in (0, 512):
            MM(p2a[:, s0:s0 + 512], ones32[:], rowsA[0:1, s0:s0 + 512], True, True,
               K("ones32") + K("rowsA"), pkh("p2a", s0, s0 + 512))
            MM(p2b[:, s0:s0 + 512], ones32[:], rowsC[0:1, s0:s0 + 512], True, True,
               K("ones32") + K("rowsC"), pkh("p2b", s0, s0 + 512))
        sch.add("dv", lambda e: e.tensor_copy(out=mubc[:], in_=p2a[:, 0:1024]),
                reads=PKEY["p2a"], writes=K("mubc"))
        sch.add("ac", lambda e: e.copy(out=rstdbc[:], in_=p2b[:, 0:1024]),
                reads=PKEY["p2b"], writes=K("rstdbc"))
        for c in range(NCH):
            csl = slice(c * 1024, (c + 1) * 1024)
            sk = K("s32", c * 1024, (c + 1) * 1024)
            TT("dv", ALU.subtract, s32[:, csl], s32[:, csl], mubc[:], sk + K("mubc"), sk)
            TT("dv", ALU.mult, s32[:, csl], s32[:, csl], rstdbc[:], sk + K("rstdbc"), sk)
            if emit_bf16:
                CAST("dv", s16[:, csl], s32[:, csl], sk, K("s16", c * 1024, (c + 1) * 1024))

    layernorm(True)
    if "xln" in tapd:
        DMA("gp", tapd["xln"], s32[:], K("s32"), [], "tapxln")

    fuse = big[:, 0:32768]
    for s2 in range(2):
        toff = s2 * 512
        for ib in range(NIB):
            par = ib % 2
            gb = gpanb[:, par, :]
            ub = upanb[:, par, :]
            gbk = K("gpanb", par * 2048, (par + 1) * 2048)
            ubk = K("upanb", par * 2048, (par + 1) * 2048)
            gib = s2 * NIB + ib
            if gib < 2:
                DMA("gp", gb.rearrange("p (c q) -> p c q", q=128),
                    I["wg"][ib:ib + 1].rearrange("a p c q -> p (a c) q"), gbk, gbk, "gp8%d" % par)
                DMA("gp", ub.rearrange("p (c q) -> p c q", q=128),
                    I["wu"][ib:ib + 1].rearrange("a p c q -> p (a c) q"), ubk, ubk, "up8%d" % par)
            pg = p2a[:, par * 512:(par + 1) * 512]
            pgk = pkh("p2a", par * 512, (par + 1) * 512)
            pu2 = p2b[:, par * 512:(par + 1) * 512]
            puk = pkh("p2b", par * 512, (par + 1) * 512)
            for c in range(NCH):
                MM(pg, gb[:, c * 128:(c + 1) * 128],
                   s16[:, c * 1024 + toff: c * 1024 + toff + 512], c == 0, c == NCH - 1,
                   gbk + K("s16", c * 1024 + toff, c * 1024 + toff + 512), pgk)
            for c in range(NCH):
                MM(pu2, ub[:, c * 128:(c + 1) * 128],
                   s16[:, c * 1024 + toff: c * 1024 + toff + 512], c == 0, c == NCH - 1,
                   ubk + K("s16", c * 1024 + toff, c * 1024 + toff + 512), puk)
            nx = gib + 2
            if nx < 2 * NIB:
                nib_, npar = nx % NIB, nx % 2
                ngb = gpanb[:, npar, :]
                nub = upanb[:, npar, :]
                ngbk = K("gpanb", npar * 2048, (npar + 1) * 2048)
                nubk = K("upanb", npar * 2048, (npar + 1) * 2048)
                DMA("gp", ngb.rearrange("p (c q) -> p c q", q=128),
                    I["wg"][nib_:nib_ + 1].rearrange("a p c q -> p (a c) q"), ngbk, ngbk, "gp8%d" % npar)
                DMA("gp", nub.rearrange("p (c q) -> p c q", q=128),
                    I["wu"][nib_:nib_ + 1].rearrange("a p c q -> p (a c) q"), nubk, nubk, "up8%d" % npar)
            gv = gbuf[:, par, :]
            gvk = K("gbuf", par * 512, (par + 1) * 512)
            sch.add("ac", lambda e, o=gv, i=pg, sc=sgc[:, ib:ib + 1]:
                    e.activation(out=o, in_=i, func=AF.Silu, scale=sc),
                    reads=pgk + K("sgc"), writes=gvk)
            sch.add("dv", lambda e, o=fuse[:, ib * 512:(ib + 1) * 512], i=pu2, sc=suc[:, ib:ib + 1], gg=gv:
                    e.scalar_tensor_tensor(out=o, in0=i, scalar=sc, in1=gg, op0=ALU.mult, op1=ALU.mult),
                    reads=puk + K("suc") + gvk, writes=K("big", ib * 512, (ib + 1) * 512))
        for ob in range(NCH):
            pd = p2c[:, (ob % 2) * 512:((ob % 2) + 1) * 512]
            pdk = pkh("p2c", (ob % 2) * 512, ((ob % 2) + 1) * 512)
            for half in range(2):
                hp = (ob * 2 + half + s2) % 2
                dv_ = wdb[:, hp, :]
                dvk = K("wdb", hp * 4096, (hp + 1) * 4096)
                DMA("gp", dv_.rearrange("p (c q) -> p c q", q=128),
                    I["wd"][ob:ob + 1].rearrange("a p c q -> p (a c) q")[:, half * 32:(half + 1) * 32, :],
                    dvk, dvk, "wd%d" % hp)
                for u in range(32):
                    c = half * 32 + u
                    MM(pd, dv_[:, u * 128:(u + 1) * 128], fuse[:, c * 512:(c + 1) * 512],
                       c == 0, c == NIB - 1,
                       dvk + K("big", c * 512, (c + 1) * 512), pdk)
            sch.add("dv", lambda e, o=s32[:, ob * 1024 + toff: ob * 1024 + toff + 512],
                    i=pd, sc=sdc[:, ob:ob + 1]:
                    e.scalar_tensor_tensor(out=o, in0=i, scalar=sc, in1=o, op0=ALU.mult, op1=ALU.add),
                    reads=pdk + K("sdc") + K("s32", ob * 1024 + toff, ob * 1024 + toff + 512),
                    writes=K("s32", ob * 1024 + toff, ob * 1024 + toff + 512))
    if "r2" in tapd:
        DMA("gp", tapd["r2"], s32[:], K("s32"), [], "tapr2")

    layernorm(False)
    out_dma = DMA("gp", yout.rearrange("c p t -> p c t"),
                  s32[:].rearrange("p (c t) -> p c t", t=T), K("s32"), [], "yout")
    sch.add("gp", lambda e: e.engine_nop(), deps=(out_dma,))

    cnt = sch.finalize()
    nc._sched = sch
    with ExitStack() as stack:
        E = stack.enter_context
        sems = {}
        for key in sorted(cnt):
            sems[key] = E(nc.semaphore("sem_" + key.replace(":", "_")))
        block = E(nc.Block())

        @block.tensor
        def _(eng):
            sch.emit("pe", eng, sems)

        @block.vector
        def _(eng):
            sch.emit("dv", eng, sems)

        @block.scalar
        def _(eng):
            sch.emit("ac", eng, sems)

        @block.sync
        def _(eng):
            sch.emit("sy", eng, sems)

        @block.gpsimd
        def _(eng):
            sch.emit("gp", eng, sems)

    return nc


def kernel(**inputs):
    in_maps, own_all = host_prep(inputs)
    nc = build_program()
    res = run_bass_kernel_spmd(nc, in_maps, core_ids=list(range(8)))
    y = np.zeros((B, S, H), dtype=np.float32)
    for c in range(8):
        b, own = own_all[c]
        yc = np.asarray(res.results[c]["yout"])
        y[b, own] = yc.reshape(H, T).T
    return y


def run_dev(inputs, taps=()):
    in_maps, own_all = host_prep(inputs)
    nc = build_program(taps=taps)
    res = run_bass_kernel_spmd(nc, in_maps, core_ids=list(range(8)))
    return res, own_all


# version 11 (final)
# speedup vs baseline: 1.0512x; 1.0159x over previous
import math
import os
from types import SimpleNamespace

import numpy as np
import ml_dtypes

import concourse.bass as bass
import concourse.mybir as mybir
from concourse.bass_utils import run_bass_kernel_spmd
from contextlib import ExitStack

F32 = mybir.dt.float32
BF16 = mybir.dt.bfloat16
I8 = mybir.dt.int8
AF = mybir.ActivationFunctionType
ALU = mybir.AluOpType
AXX = mybir.AxisListType.X

B, S, H = 2, 4096, 2048
NQ, NKV, D, RANK = 16, 8, 128, 8
INTER = 8192
NCH = H // 128
SLOTS = 8
T = SLOTS * 128
KB = S // 128
TC = 8
NIB = INTER // 128
SCALE = 1.0 / math.sqrt(D)
EPS = 1e-5

PAGE = 2048
BF2 = np.dtype(ml_dtypes.bfloat16)
SAFE_SYNC = os.environ.get("BASS_SIM_FP64") == "1"


class Sched:
    ENGS = ("pe", "dv", "ac", "sy", "gp")

    def __init__(self):
        self.ops = []
        self.lastw = {}
        self.readers = {}

    def add(self, eng, fn, reads=(), writes=(), dma=False, deps=(), stream=None, tag=""):
        op = SimpleNamespace(i=len(self.ops), eng=eng, fn=fn, dma=dma,
                             deps=set(), need_inc=False, cum=0, clock=None,
                             semkey=None, waits=None, stream=stream, tag=tag)
        for k in reads:
            w = self.lastw.get(k)
            if w is not None:
                op.deps.add(w.i)
            self.readers.setdefault(k, []).append(op)
        for k in writes:
            w = self.lastw.get(k)
            if w is not None:
                op.deps.add(w.i)
            for rop in self.readers.get(k, ()):
                op.deps.add(rop.i)
            self.lastw[k] = op
            self.readers[k] = []
        for d in deps:
            op.deps.add(d.i)
        op.deps.discard(op.i)
        self.ops.append(op)
        return op

    def finalize(self):
        for op in self.ops:
            if op.dma:
                op.need_inc = True
        for op in self.ops:
            for d in op.deps:
                dop = self.ops[d]
                if dop.dma or dop.eng != op.eng or SAFE_SYNC:
                    dop.need_inc = True
        cnt = {}
        for op in self.ops:
            op.semkey = ("D:" + (op.stream or op.eng)) if op.dma else op.eng
            if op.need_inc:
                cnt[op.semkey] = cnt.get(op.semkey, 0) + (16 if op.dma else 1)
                op.cum = cnt[op.semkey]
        for op in self.ops:
            cl = {}
            for d in op.deps:
                for k, v in self.ops[d].clock.items():
                    if cl.get(k, 0) < v:
                        cl[k] = v
            if op.need_inc and op.cum:
                cl[op.semkey] = op.cum
            op.clock = cl
        eclock = {e: {} for e in self.ENGS}
        self.per_eng = {e: [] for e in self.ENGS}
        for op in self.ops:
            ec = eclock[op.eng]
            waits = []
            for d in sorted(op.deps):
                dop = self.ops[d]
                if dop.eng == op.eng and not dop.dma and not SAFE_SYNC:
                    continue
                if dop.cum and ec.get(dop.semkey, 0) < dop.cum:
                    waits.append((dop.semkey, dop.cum))
                    for k, v in dop.clock.items():
                        if ec.get(k, 0) < v:
                            ec[k] = v
            op.waits = waits
            self.per_eng[op.eng].append(op)
        return cnt

    def emit(self, eng_name, engobj, sems):
        for op in self.per_eng[eng_name]:
            for sk, val in op.waits:
                engobj.wait_ge(sems[sk], val)
            inst = op.fn(engobj)
            if op.need_inc:
                assert inst is not None, f"op {op.i} on {op.eng} needs inc but returned no instruction"
                inst.then_inc(sems[op.semkey], 16 if op.dma else 1)


class Arena:

    def __init__(self, nc, nbytes):
        self.nc = nc
        self.nbytes = nbytes
        self.handle = nc.alloc_sbuf_tensor("arena", [128, nbytes], mybir.dt.uint8)
        self.base = nc.lookup_mloc(self.handle).addr
        self.tensors = {}

    def tensor(self, name, off, shape, dtype):
        size = int(np.prod(shape[1:])) * mybir.dt.size(dtype)
        assert off % 32 == 0 and off + size <= self.nbytes, (name, off, size, self.nbytes)
        h = self.nc.virtual_sbuf_tensor(name, list(shape), dtype)
        self.nc._tensor(h.name, list(shape), dtype, type="SB")
        mloc = self.nc.lookup_mloc(h)
        mloc.allocated = not self.nc.target_bir_lowering
        mloc.addr = self.base + off
        self.tensors[name] = (h, off, size)
        return h

    def keys(self, name, el_lo=None, el_hi=None):
        h, off, size = self.tensors[name]
        es = mybir.dt.size(h.dtype)
        lo = off + (0 if el_lo is None else el_lo * es)
        hi = off + (size if el_hi is None else el_hi * es)
        return [("S", p) for p in range(lo // PAGE, (hi + PAGE - 1) // PAGE)]


def _perm_head():
    return np.concatenate([np.arange(0, D, 2), np.arange(1, D, 2)])


def _rope_tables(positions):
    j = np.arange(0, D, 2, dtype=np.float32)
    inv = (1.0 / (10000.0 ** (j / np.float32(D)))).astype(np.float32)
    t = positions.astype(np.float32)
    freqs = (inv[:, None] * t[None, :]).astype(np.float32)
    c = np.cos(freqs.astype(np.float64)).astype(np.float32)
    s = np.sin(freqs.astype(np.float64)).astype(np.float32)
    tabC = np.ascontiguousarray(np.concatenate([c, c], axis=0))
    tabS = np.ascontiguousarray(np.concatenate([s, -s], axis=0))
    return tabC, tabS


def _tile_lhsT(wT, nblk):
    K_in = wT.shape[0]
    a = np.asarray(wT).reshape(K_in // 128, 128, nblk, 128)
    return np.ascontiguousarray(a.transpose(2, 1, 0, 3))


def host_prep(inp):
    perm = _perm_head()
    perm_q = np.concatenate([h * D + perm for h in range(NQ)])
    perm_k = np.concatenate([h * D + perm for h in range(NKV)])

    x = np.asarray(inp["L0_attn_x"], dtype=np.float32)
    WqT = np.ascontiguousarray(np.asarray(inp["L0_attn_Wq"], dtype=np.float32).T[:, perm_q])
    WkT = np.ascontiguousarray(np.asarray(inp["L0_attn_Wk"], dtype=np.float32).T[:, perm_k])
    WvT = np.ascontiguousarray(np.asarray(inp["L0_attn_Wv"], dtype=np.float32).T)
    WoT = np.ascontiguousarray(np.asarray(inp["L0_attn_Wo"], dtype=np.float32).T)

    shared = {
        "wq": _tile_lhsT(WqT, NCH),
        "wk": _tile_lhsT(WkT, NKV),
        "wv": np.ascontiguousarray(WvT.reshape(NCH, 128, NKV * 128).transpose(1, 0, 2)),
        "wo": _tile_lhsT(WoT, NCH),
        "la4": np.ascontiguousarray(np.stack(
            [np.asarray(inp[k], dtype=np.float32).T.reshape(NCH, 128, RANK).transpose(1, 0, 2).reshape(128, NCH * RANK)
             for k in ("L0_attn_qA", "L0_attn_kA", "L0_attn_vA", "L0_attn_oA")], axis=1)),
        "qb": np.ascontiguousarray(np.asarray(inp["L0_attn_qB"], dtype=np.float32).T[:, perm_q]),
        "kb": np.ascontiguousarray(np.asarray(inp["L0_attn_kB"], dtype=np.float32).T[:, perm_k]),
        "vb": np.ascontiguousarray(np.asarray(inp["L0_attn_vB"], dtype=np.float32).T),
        "ob": np.ascontiguousarray(np.asarray(inp["L0_attn_oB"], dtype=np.float32).T),
        "wg": _tile_lhsT(np.asarray(inp["L0_ffn_w_gate"]), NIB),
        "wu": _tile_lhsT(np.asarray(inp["L0_ffn_w_up"]), NIB),
        "wd": _tile_lhsT(np.asarray(inp["L0_ffn_w_down"]), NCH),
        "sg": np.ascontiguousarray(np.asarray(inp["L0_ffn_s_gate"], dtype=np.float32).reshape(NIB, 128).T),
        "su": np.ascontiguousarray(np.asarray(inp["L0_ffn_s_up"], dtype=np.float32).reshape(NIB, 128).T),
        "sd": np.ascontiguousarray(np.asarray(inp["L0_ffn_s_down"], dtype=np.float32).reshape(NCH, 128).T),
    }
    kc, ks = _rope_tables(np.arange(S))
    shared["ropekc"] = kc
    shared["ropeks"] = ks

    xbt = []
    for b in range(B):
        xbt.append(np.ascontiguousarray(x[b].T.reshape(NCH, 128, S)))

    in_maps, own_all = [], []
    for c in range(8):
        b, r = divmod(c, 4)
        blocks = [4 * (e - 1) + r for e in range(SLOTS, 0, -1)]
        own = np.concatenate([np.arange(j * 128, (j + 1) * 128) for j in blocks])
        own_all.append((b, own))
        xo = np.ascontiguousarray(x[b][own].T)
        qc, qs = _rope_tables(own)
        m = np.zeros((128, SLOTS, 4, 128), dtype=BF2)
        kk = np.arange(128)[:, None]
        qq = np.arange(128)[None, :]
        for sp in range(SLOTS):
            for jj in range(4):
                if jj < r:
                    m[:, sp, jj, :] = 1.0
                elif jj == r:
                    m[:, sp, jj, :] = (kk <= qq).astype(BF2)
        in_maps.append(dict(
            shared,
            xbt=xbt[b],
            xot=np.ascontiguousarray(xo.reshape(NCH, 128, T)),
            ropeqc=qc, ropeqs=qs,
            masks=np.ascontiguousarray(m.reshape(128, SLOTS * 4 * 128)),
        ))
    return in_maps, own_all


def build_program(taps=()):
    nc = bass.Bass("TRN2", target_bir_lowering=False)

    I = {}
    def din(name, shape, dtype):
        I[name] = nc.dram_tensor(name, list(shape), dtype, kind="ExternalInput").ap()

    din("xbt", [NCH, 128, S], F32)
    din("xot", [NCH, 128, T], F32)
    din("wq", [NCH, 128, NCH, 128], F32)
    din("wk", [NKV, 128, NCH, 128], F32)
    din("wv", [128, NCH, NKV * 128], F32)
    din("wo", [NCH, 128, NCH, 128], F32)
    din("la4", [128, 4, NCH * RANK], F32)
    din("qb", [RANK, NQ * 128], F32)
    din("kb", [RANK, NKV * 128], F32)
    din("vb", [RANK, NKV * 128], F32)
    din("ob", [RANK, NCH * 128], F32)
    din("wg", [NIB, 128, NCH, 128], I8)
    din("wu", [NIB, 128, NCH, 128], I8)
    din("wd", [NCH, 128, NIB, 128], I8)
    din("sg", [128, NIB], F32)
    din("su", [128, NIB], F32)
    din("sd", [128, NCH], F32)
    din("ropekc", [128, S], F32)
    din("ropeks", [128, S], F32)
    din("ropeqc", [128, T], F32)
    din("ropeqs", [128, T], F32)
    din("masks", [128, SLOTS * 4 * 128], BF16)

    yout = nc.dram_tensor("yout", [NCH, 128, T], F32, kind="ExternalOutput").ap()
    tk = lambda n: "ExternalOutput" if n in taps else "Internal"
    kspill = nc.dram_tensor("kspill", [NKV, 128, S], BF16, kind=tk("kspill")).ap()
    vspill = nc.dram_tensor("vspill", [2, 128, KB * 512], BF16, kind=tk("vspill")).ap()
    tapd = {}
    for nm in taps:
        if nm in ("kspill", "vspill"):
            continue
        if nm == "dbg":
            tapd[nm] = nc.dram_tensor("tap_dbg", [128, 64], F32, kind="ExternalOutput").ap()
            continue
        dt = BF16 if nm in ("qt", "ot") else F32
        tapd[nm] = nc.dram_tensor("tap_" + nm, [128, NCH * 1024], dt, kind="ExternalOutput").ap()

    AR = Arena(nc, 206 * 1024)
    DOF = 163840
    s32 = AR.tensor("s32", 0, [128, NCH * 1024], F32)
    s16 = AR.tensor("s16", 65536, [128, NCH * 1024], BF16)
    big = AR.tensor("big", 98304, [128, 32768], BF16)
    xcb = AR.tensor("xcb", 32768, [128, 2, NCH * 512], BF16)
    kvstage = AR.tensor("kvstage", 0, [128, 4, 512], BF16)
    vt_stage = AR.tensor("vt_stage", 4096, [128, 512], BF16)
    xotb = AR.tensor("xotb", 98304, [128, NCH * 1024], BF16)

    def At(name, off, shape, dtype):
        return AR.tensor(name, DOF + off, list(shape), dtype)

    gpanb = At("gpanb", 0, [128, 2, 2048], BF16)
    wpanb = At("wpanb", 8192, [128, 2, 2048], BF16)
    ropec = At("ropec", 16384, [128, 1024], F32)
    ropes = At("ropes", 20480, [128, 1024], F32)
    qscr = At("qscr", 24576, [128, 1024], F32)
    qscr2 = At("qscr2", 28672, [128, 1024], F32)
    sqa = At("sqa", 32768, [RANK, 1024], BF16)
    sqsq = At("sqsq", 34816, [128, 1024], BF16)
    maskv = At("maskv", 16384, [128, SLOTS * 4 * 128], BF16)
    ptile = At("ptile", 24576, [128, 2, 1024], BF16)
    sbc = At("sbc", 28672, [128, 1024], F32)
    mubc = At("mubc", 16384, [128, 1024], F32)
    rstdbc = At("rstdbc", 20480, [128, 1024], F32)
    rowsA = At("rowsA", 24576, [1, 1024], F32)
    rowsB = At("rowsB", 28672, [1, 1024], F32)
    rowsC = At("rowsC", 32768, [1, 1024], F32)
    wdb = At("wdb", 16384, [128, 2, 4096], BF16)
    gbuf = At("gbuf", 32768, [128, 2, 512], BF16)
    upanb = At("upanb", 34816, [128, 2, 2048], BF16)
    lorab = At("lorab", 38912, [RANK, 2048], BF16)
    la4b = At("la4b", 43008, [128, 4, 128], BF16)
    onesb = At("onesb", 44032, [128, 1], BF16)
    ones32 = At("ones32", 44064, [1, 128], F32)
    negc = At("negc", 44576, [128, 16], F32)
    kmaxsq = At("kmaxsq", 44640, [1, 16], F32)
    qmaxsq = At("qmaxsq", 44704, [1, 8], F32)
    qmaxs2 = At("qmaxs2", 44736, [1, 8], F32)
    scrow = At("scrow", 44768, [1, 8], F32)
    scrow2 = At("scrow2", 44800, [1, 8], F32)
    sgc = At("sgc", 44832, [128, NIB], F32)
    suc = At("suc", 45088, [128, NIB], F32)
    sdc = At("sdc", 45344, [128, NCH], F32)
    cst = At("cst", 45408, [128, 4], F32)

    K = AR.keys

    p2a = nc.alloc_psum_tensor("p2a", [128, 1024], F32)
    p2b = nc.alloc_psum_tensor("p2b", [128, 1024], F32)
    p2c = nc.alloc_psum_tensor("p2c", [128, 1024], F32)
    prow = nc.alloc_psum_tensor("prow", [1, 1024], F32)
    PKEY = {"p2a": [("P", 0), ("P", 1)], "p2b": [("P", 2), ("P", 3)],
            "p2c": [("P", 4), ("P", 5)], "prow": [("P", 6), ("P", 7)]}

    def pkh(name, lo_el, hi_el):
        ks = PKEY[name]
        out = []
        if lo_el < 512:
            out.append(ks[0])
        if hi_el > 512:
            out.append(ks[1])
        return out

    sch = Sched()

    def MM(out_ap, lhsT_ap, rhs_ap, start, stop, reads, writes):
        sch.add("pe", lambda e, o=out_ap, l=lhsT_ap, r=rhs_ap, s1=start, s2=stop:
                e.matmul(o, l, r, start=s1, stop=s2), reads=reads, writes=writes)

    def DMA(eng, out_ap, in_ap, reads, writes, stream):
        return sch.add(eng, lambda e, o=out_ap, i=in_ap: e.dma_start(out=o, in_=i),
                       reads=reads, writes=writes, dma=True, stream=stream)

    def CAST(eng, out_ap, in_ap, reads, writes):
        if eng == "ac":
            sch.add(eng, lambda e, o=out_ap, i=in_ap: e.copy(out=o, in_=i),
                    reads=reads, writes=writes)
        else:
            sch.add(eng, lambda e, o=out_ap, i=in_ap: e.tensor_copy(out=o, in_=i),
                    reads=reads, writes=writes)

    def TT(eng, op, out_ap, a_ap, b_ap, reads, writes):
        sch.add(eng, lambda e, o=out_ap, a=a_ap, b=b_ap, al=op:
                e.tensor_tensor(out=o, in0=a, in1=b, op=al), reads=reads, writes=writes)

    def panel_ap(inp, j):
        return I[inp][j:j + 1].rearrange("a p c q -> p (a c) q").rearrange("p c q -> p (c q)")

    sch.add("dv", lambda e: e.memset(onesb[:], 1.0), writes=K("onesb"))
    sch.add("dv", lambda e: e.memset(ones32[:], 1.0), writes=K("ones32"))
    sch.add("dv", lambda e: e.memset(kmaxsq[:], 0.0), writes=K("kmaxsq"))
    sch.add("dv", lambda e: e.memset(cst[:, 0:1], EPS), writes=K("cst"))
    sch.add("dv", lambda e: e.memset(cst[:, 1:2], 1.0 / D), writes=K("cst"))
    sch.add("dv", lambda e: e.memset(cst[:, 2:3], SCALE), writes=K("cst"))
    DMA("sy", sgc[:], I["sg"], [], K("sgc"), "scg")
    DMA("sy", suc[:], I["su"], [], K("suc"), "scu")
    DMA("sy", sdc[:], I["sd"], [], K("sdc"), "scd")
    DMA("sy", ropec[:], I["ropeqc"], [], K("ropec"), "ropec")
    DMA("sy", ropes[:], I["ropeqs"], [], K("ropes"), "ropes")
    DMA("gp", xotb[:].rearrange("p (c t) -> p c t", t=1024), I["xot"].rearrange("c p t -> p c t"),
        K("xotb"), K("xotb"), "xotb")
    DMA("gp", wpanb[:, 0, :], panel_ap("wq", 0), K("wpanb", 0, 2048), K("wpanb", 0, 2048), "panq0")
    DMA("gp", wpanb[:, 1, :], panel_ap("wq", 1), K("wpanb", 2048, 4096), K("wpanb", 2048, 4096), "panq1")
    DMA("gp", la4b[:], I["la4"], K("la4b"), K("la4b"), "laA")
    laab = la4b[:, 0, :]
    laabK = la4b[:, 1, :]
    laab2 = la4b[:, 2, :]
    laabO = la4b[:, 3, :]
    DMA("gp", lorab[:], I["qb"], K("lorab"), K("lorab"), "lqb")

    pend_qnorms = []
    for bq in range(NQ):
        par = bq % 2
        wb = wpanb[:, par, :]
        wbk = K("wpanb", par * 2048, (par + 1) * 2048)
        pq, pqn = (p2a, "p2a") if par == 0 else (p2b, "p2b")
        for sH in range(2):
            sl = slice(sH * 512, (sH + 1) * 512)
            for c in range(NCH):
                MM(pq[:, sl], wb[:, c * 128:(c + 1) * 128],
                   xotb[:, c * 1024 + sH * 512: c * 1024 + (sH + 1) * 512],
                   c == 0, False,
                   wbk + K("xotb", c * 1024 + sH * 512, c * 1024 + (sH + 1) * 512),
                   pkh(pqn, sH * 512, (sH + 1) * 512))
        if bq == 0:
            for sH2 in range(2):
                for c in range(NCH):
                    MM(p2b[0:RANK, sH2 * 512:(sH2 + 1) * 512], laab[:, c * RANK:(c + 1) * RANK],
                       xotb[:, c * 1024 + sH2 * 512: c * 1024 + (sH2 + 1) * 512],
                       c == 0, c == NCH - 1,
                       K("la4b") + K("xotb", c * 1024 + sH2 * 512, c * 1024 + (sH2 + 1) * 512),
                       pkh("p2b", sH2 * 512, (sH2 + 1) * 512))
            sch.add("dv", lambda e: e.tensor_scalar_mul(out=sqa[:], in0=p2b[0:RANK, :], scalar1=2.0),
                    reads=PKEY["p2b"], writes=K("sqa"))
        for sH in range(2):
            sl = slice(sH * 512, (sH + 1) * 512)
            MM(pq[:, sl], lorab[0:RANK, bq * 128:(bq + 1) * 128], sqa[0:RANK, sl],
               False, True, K("lorab") + K("sqa"), pkh(pqn, sH * 512, (sH + 1) * 512))
        if bq + 2 < NQ:
            DMA("gp", wpanb[:, par, :], panel_ap("wq", bq + 2), wbk, wbk, "panq%d" % par)
        TT("dv", ALU.mult, qscr[:], pq[:, 0:1024], ropec[:], PKEY[pqn] + K("ropec"), K("qscr"))
        TT("dv", ALU.mult, qscr2[0:64, :], pq[64:128, 0:1024], ropes[64:128, :],
           PKEY[pqn] + K("ropes"), K("qscr2"))
        TT("dv", ALU.mult, qscr2[64:128, :], pq[0:64, 0:1024], ropes[0:64, :],
           PKEY[pqn] + K("ropes"), K("qscr2"))
        TT("dv", ALU.add, s16[:, bq * 1024:(bq + 1) * 1024], qscr[:], qscr2[:],
           K("qscr") + K("qscr2"), K("s16", bq * 1024, (bq + 1) * 1024))
        if len(pend_qnorms) >= 2:
            pend_qnorms.pop(0)()
        def qnorm(bq=bq):
            prwk = PKEY["prow"] if bq % 2 == 0 else PKEY["p2c"]
            sch.add("ac", lambda e, i=s16[:, bq * 1024:(bq + 1) * 1024], o=sqsq[:]:
                    e.activation(out=o, in_=i, func=AF.Square),
                    reads=K("s16", bq * 1024, (bq + 1) * 1024), writes=K("sqsq"))
            for sH in range(2):
                out = prow[0:1, sH * 512:(sH + 1) * 512] if bq % 2 == 0 else p2c[0:1, sH * 512:(sH + 1) * 512]
                MM(out, onesb[:], sqsq[:, sH * 512:(sH + 1) * 512],
                   True, True, K("onesb") + K("sqsq"), [prwk[sH]])
            qdst = qmaxsq if bq % 2 == 0 else qmaxs2
            src = prow[0:1, 0:1024] if bq % 2 == 0 else p2c[0:1, 0:1024]
            sch.add("dv", lambda e, o=qdst[0:1, bq // 2:bq // 2 + 1], i=src:
                    e.reduce_max(out=o, in_=i, axis=AXX),
                    reads=prwk, writes=K("qmaxsq") + K("qmaxs2"))
        pend_qnorms.append(qnorm)
    while pend_qnorms:
        pend_qnorms.pop(0)()
    if "qt" in tapd:
        DMA("gp", tapd["qt"], s16[:], K("s16"), [], "tapqt")

    DMA("gp", lorab[0:RANK, 0:1024], I["kb"], K("lorab", 0, 1024), K("lorab", 0, 1024), "lkb")
    DMA("gp", lorab[0:RANK, 1024:2048], I["vb"], K("lorab", 1024, 2048), K("lorab", 1024, 2048), "lvb")
    for kh in range(NKV):
        DMA("gp", big[:, kh * 2048:(kh + 1) * 2048], panel_ap("wk", kh),
            K("big", kh * 2048, (kh + 1) * 2048), K("big", kh * 2048, (kh + 1) * 2048), "wk%d" % kh)
    for cp in range(NCH // 2):
        DMA("gp", big[:, 16384 + cp * 2048:16384 + (cp + 1) * 2048].rearrange("p (a t) -> p a t", a=2),
            I["wv"][:, 2 * cp:2 * cp + 2, :],
            K("big", 16384 + cp * 2048, 16384 + (cp + 1) * 2048),
            K("big", 16384 + cp * 2048, 16384 + (cp + 1) * 2048), "wv%d" % cp)
    wkb = big[:, 0:16384]
    wvb = big[:, 16384:32768]

    pend_knorms = []
    for tc in range(TC):
        tsl = slice(tc * 512, (tc + 1) * 512)
        par = tc % 2
        xv = xcb[:, par, :]
        xvk = K("xcb", par * NCH * 512, (par + 1) * NCH * 512)
        DMA("gp", xv.rearrange("p (c t) -> p c t", t=512),
            I["xbt"][:, :, tsl].rearrange("c p t -> p c t"), xvk, xvk, "xcb%d" % par)
        DMA("sy", ropec[:, 0:512], I["ropekc"][:, tsl], K("ropec"), K("ropec", 0, 512), "ropec")
        DMA("sy", ropes[:, 0:512], I["ropeks"][:, tsl], K("ropes"), K("ropes", 0, 512), "ropes")
        for c in range(NCH):
            MM(p2b[0:RANK, 0:512], laabK[:, c * RANK:(c + 1) * RANK], xv[:, c * 512:(c + 1) * 512],
               c == 0, c == NCH - 1, K("la4b") + xvk, pkh("p2b", 0, 512))
        sch.add("dv", lambda e: e.tensor_scalar_mul(out=sqa[0:RANK, 0:512], in0=p2b[0:RANK, 0:512], scalar1=2.0),
                reads=pkh("p2b", 0, 512), writes=K("sqa", 0, 512))
        for c in range(NCH):
            MM(p2b[0:RANK, 512:1024], laab2[:, c * RANK:(c + 1) * RANK], xv[:, c * 512:(c + 1) * 512],
               c == 0, c == NCH - 1, K("la4b") + xvk, pkh("p2b", 512, 1024))
        sch.add("dv", lambda e: e.tensor_scalar_mul(out=sqa[0:RANK, 512:1024], in0=p2b[0:RANK, 512:1024], scalar1=2.0),
                reads=pkh("p2b", 512, 1024), writes=K("sqa", 512, 1024))
        ska, skak = sqa[0:RANK, 0:512], K("sqa", 0, 512)
        sva, svak = sqa[0:RANK, 512:1024], K("sqa", 512, 1024)
        def kstep(kh):
            kp = kh % 2
            pk = p2a[:, kp * 512:(kp + 1) * 512]
            pkk = pkh("p2a", kp * 512, (kp + 1) * 512)
            for c in range(NCH):
                MM(pk, wkb[:, kh * 2048 + c * 128: kh * 2048 + (c + 1) * 128],
                   xv[:, c * 512:(c + 1) * 512], c == 0, False,
                   K("big", kh * 2048, (kh + 1) * 2048) + xvk, pkk)
            MM(pk, lorab[0:RANK, kh * 128:(kh + 1) * 128], ska, False, True,
               K("lorab", 0, 1024) + skak, pkk)
            if len(pend_knorms) >= 2:
                pend_knorms.pop(0)()
            kst = kvstage[:, kh % 4, :]
            kstk = K("kvstage", (kh % 4) * 512, (kh % 4 + 1) * 512)
            TT("dv", ALU.mult, qscr[:, 0:512], pk, ropec[:, 0:512], pkk + K("ropec", 0, 512), K("qscr", 0, 512))
            TT("dv", ALU.mult, qscr2[0:64, 0:512], pk[64:128, :], ropes[64:128, 0:512],
               pkk + K("ropes", 0, 512), K("qscr2", 0, 512))
            TT("dv", ALU.mult, qscr2[64:128, 0:512], pk[0:64, :], ropes[0:64, 0:512],
               pkk + K("ropes", 0, 512), K("qscr2", 0, 512))
            TT("dv", ALU.add, kst, qscr[:, 0:512], qscr2[:, 0:512],
               K("qscr", 0, 512) + K("qscr2", 0, 512), kstk)
            DMA("sy", kspill[kh:kh + 1, :, tsl].rearrange("a p t -> p (a t)"),
                kst, kstk, [("KS", kh, tc)], "ks")
            def knorm(kh=kh, kst=kst, kstk=kstk):
                kq = (kh % 2) * 512
                sch.add("ac", lambda e: e.activation(out=sqsq[:, kq:kq + 512], in_=kst, func=AF.Square),
                        reads=kstk, writes=K("sqsq", kq, kq + 512))
                MM(prow[0:1, kq:kq + 512], onesb[:], sqsq[:, kq:kq + 512], True, True,
                   K("onesb") + K("sqsq", kq, kq + 512), pkh("prow", kq, kq + 512))
                sch.add("dv", lambda e, o=scrow2[0:1, 0:1] if kh % 2 else scrow[0:1, 0:1]:
                        e.reduce_max(out=o, in_=prow[0:1, kq:kq + 512], axis=AXX),
                        reads=pkh("prow", kq, kq + 512), writes=K("scrow") + K("scrow2"))
                sch.add("dv", lambda e, o=kmaxsq[0:1, kh:kh + 1],
                        i=scrow2[0:1, 0:1] if kh % 2 else scrow[0:1, 0:1]:
                        e.tensor_tensor(out=o, in0=i, in1=o, op=ALU.max),
                        reads=K("scrow") + K("scrow2") + K("kmaxsq"), writes=K("kmaxsq"))
            pend_knorms.append(knorm)

        def vstep(tb, vq):
            vp = (tb * 2 + vq) % 2
            pv = p2c[:, vp * 512:(vp + 1) * 512]
            pvk = pkh("p2c", vp * 512, (vp + 1) * 512)
            for c in range(NCH):
                MM(pv, xv[:, c * 512 + tb * 128: c * 512 + (tb + 1) * 128],
                   wvb[:, c * 1024 + vq * 512: c * 1024 + (vq + 1) * 512],
                   c == 0, False,
                   xvk + K("big", 16384 + c * 1024 + vq * 512, 16384 + c * 1024 + (vq + 1) * 512), pvk)
            MM(pv, sva[0:RANK, tb * 128:(tb + 1) * 128],
               lorab[0:RANK, 1024 + vq * 512: 1024 + (vq + 1) * 512],
               False, True, svak + K("lorab", 1024, 2048), pvk)
            CAST("dv", vt_stage[:], pv, pvk, K("vt_stage"))
            DMA("sy", vspill[vq:vq + 1, :, (tc * 4 + tb) * 512:(tc * 4 + tb + 1) * 512]
                .rearrange("a p t -> p (a t)"),
                vt_stage[:], K("vt_stage"), [("VS", vq, tc * 4 + tb)], "vs")

        vlist = [(tb, vq) for tb in range(4) for vq in range(2)]
        for i in range(NKV):
            kstep(i)
            vstep(*vlist[i])
        while pend_knorms:
            pend_knorms.pop(0)()
    for qsrc, sdst in ((qmaxsq, scrow), (qmaxs2, scrow2)):
        sch.add("dv", lambda e, a=qsrc, o=sdst: e.tensor_tensor(
                out=o[0:1, 0:8], in0=a[0:1, 0:8], in1=kmaxsq[0:1, 0:8], op=ALU.mult),
                reads=K("qmaxsq") + K("qmaxs2") + K("kmaxsq"), writes=K("scrow") + K("scrow2"))
        sch.add("ac", lambda e, o=sdst: e.activation(out=o[0:1, 0:8], in_=o[0:1, 0:8],
                                                     func=AF.Sqrt, scale=cst[0:1, 1:2]),
                reads=K("scrow") + K("scrow2") + K("cst"), writes=K("scrow") + K("scrow2"))
        sch.add("dv", lambda e, o=sdst: e.tensor_scalar_mul(out=o[0:1, 0:8], in0=o[0:1, 0:8],
                                                            scalar1=-1.0),
                reads=K("scrow") + K("scrow2"), writes=K("scrow") + K("scrow2"))
    MM(p2c[:, 0:8], ones32[:], scrow[0:1, 0:8], True, True,
       K("ones32") + K("scrow"), pkh("p2c", 0, 8))
    MM(p2c[:, 8:16], ones32[:], scrow2[0:1, 0:8], True, True,
       K("ones32") + K("scrow2"), pkh("p2c", 0, 16))
    sch.add("dv", lambda e: e.tensor_copy(out=negc[:], in_=p2c[:, 0:16]),
            reads=pkh("p2c", 0, 16), writes=K("negc"))
    if "dbg" in tapd:
        DMA("gp", tapd["dbg"][:, 0:16], negc[:], K("negc"), [], "tapd0")
        DMA("gp", tapd["dbg"][0:1, 16:32], kmaxsq[:], K("kmaxsq"), [], "tapd1")
        DMA("gp", tapd["dbg"][0:1, 32:40], qmaxsq[:], K("qmaxsq"), [], "tapd2")
        DMA("gp", tapd["dbg"][0:1, 40:48], qmaxs2[:], K("qmaxs2"), [], "tapd3")

    DMA("sy", maskv[:], I["masks"], K("maskv"), K("maskv"), "mask")
    DMA("sy", s32[:].rearrange("p (c t) -> p c t", t=T), I["xot"].rearrange("c p t -> p c t"),
        K("s32"), K("s32"), "xot")
    DMA("gp", lorab[0:RANK, :], I["ob"], K("lorab"), K("lorab"), "lob")

    OT = big[:, 0:16384]
    stv = [0]

    def st_step(kh, h, kb):
        kp = kh % 2
        kbuf = big[:, 16384 + kp * 4096: 16384 + (kp + 1) * 4096]
        kbk = K("big", 16384 + kp * 4096, 16384 + (kp + 1) * 4096)
        n = 8 - kb // 4
        nw = n * 128
        stp = stv[0]
        pst, pstn = (p2a, "p2a") if stp == 0 else (p2b, "p2b")
        for s0 in range(0, nw, 512):
            s1 = min(s0 + 512, nw)
            MM(pst[:, s0:s1], kbuf[:, kb * 128:(kb + 1) * 128],
               s16[:, h * 1024 + s0: h * 1024 + s1], True, True,
               kbk + K("s16", h * 1024 + s0, h * 1024 + s1), pkh(pstn, s0, s1))
        stv[0] ^= 1
        return pst, pstn, n, nw

    def fin_step(kh, h, kb, pst, pstn, n, nw):
        kp = kh % 2
        vbuf = big[:, 24576 + kp * 4096: 24576 + (kp + 1) * 4096]
        vbk = K("big", 24576 + kp * 4096, 24576 + (kp + 1) * 4096)
        pp = kb % 2
        ptv = ptile[:, pp, :]
        qp = (h % 2) * 8 + h // 2
        rd = pkh(pstn, 0, nw) + K("negc") + K("cst")
        sch.add("ac", lambda e, o=ptv[:, 0:nw], i=pst[:, 0:nw], bi=negc[:, qp:qp + 1]:
                e.activation(out=o, in_=i, func=AF.Exp, bias=bi, scale=cst[:, 2:3]),
                reads=rd, writes=K("ptile", pp * 1024, pp * 1024 + nw))
        sp, jj = kb // 4, kb % 4
        mcol = (sp * 4 + jj) * 128
        TT("dv", ALU.mult, ptv[:, nw - 128:nw], ptv[:, nw - 128:nw],
           maskv[:, mcol:mcol + 128],
           K("ptile", pp * 1024 + nw - 128, pp * 1024 + nw) + K("maskv", mcol, mcol + 128),
           K("ptile", pp * 1024 + nw - 128, pp * 1024 + nw))
        for s0 in range(0, nw, 512):
            s1 = min(s0 + 512, nw)
            MM(prow[0:1, s0:s1], onesb[:], ptv[:, s0:s1],
               kb == 0, (kb == KB - 1) if s0 == 0 else (kb == 15),
               K("onesb") + K("ptile", pp * 1024 + s0, pp * 1024 + s1),
               pkh("prow", s0, s1))
            MM(p2c[:, s0:s1], vbuf[:, kb * 128:(kb + 1) * 128], ptv[:, s0:s1],
               kb == 0, (kb == KB - 1) if s0 == 0 else (kb == 15),
               vbk + K("ptile", pp * 1024 + s0, pp * 1024 + s1),
               pkh("p2c", s0, s1))

    def head_tail(h):
        sch.add("ac", lambda e: e.activation(out=sbc[0:1, 0:1024], in_=prow[0:1, 0:1024],
                                             func=AF.Ln),
                reads=PKEY["prow"], writes=K("sbc", 0, 1024))
        sch.add("ac", lambda e: e.activation(out=sbc[0:1, 0:1024], in_=sbc[0:1, 0:1024],
                                             func=AF.Exp, scale=-1.0),
                reads=K("sbc", 0, 1024), writes=K("sbc", 0, 1024))
        stp = stv[0]
        bps, bkn = (p2a, "p2a") if stp == 0 else (p2b, "p2b")
        for s0 in (0, 512):
            MM(bps[:, s0:s0 + 512], ones32[:], sbc[0:1, s0:s0 + 512], True, True,
               K("ones32") + K("sbc", 0, 1024), pkh(bkn, s0, s0 + 512))
        sch.add("ac", lambda e, p=bps: e.copy(out=sbc[:], in_=p[:, 0:1024]),
                reads=PKEY[bkn], writes=K("sbc"))
        sch.add("dv", lambda e, o=OT[:, h * 1024:(h + 1) * 1024]:
                e.tensor_tensor(out=o, in0=p2c[:, 0:1024], in1=sbc[:], op=ALU.mult),
                reads=PKEY["p2c"] + K("sbc"), writes=K("big", h * 1024, (h + 1) * 1024))

    steps = []
    for kh in range(NKV):
        for h in (2 * kh, 2 * kh + 1):
            for kb in range(KB):
                steps.append((kh, h, kb))
    pend = []
    for kh in range(NKV):
        kp = kh % 2
        kbuf = big[:, 16384 + kp * 4096: 16384 + (kp + 1) * 4096]
        kbk = K("big", 16384 + kp * 4096, 16384 + (kp + 1) * 4096)
        vbuf = big[:, 24576 + kp * 4096: 24576 + (kp + 1) * 4096]
        vbk = K("big", 24576 + kp * 4096, 24576 + (kp + 1) * 4096)
        DMA("gp", kbuf, kspill[kh:kh + 1].rearrange("a p t -> p (a t)"),
            [("KS", kh, t_) for t_ in range(TC)] + kbk, kbk, "kbuf%d" % kp)
        vq, w = kh // 4, kh % 4
        vsrc = (vspill[vq:vq + 1].rearrange("a p t -> p (a t)")
                .rearrange("p (tb d) -> p tb d", d=512)[:, :, w * 128:(w + 1) * 128])
        DMA("gp", vbuf.rearrange("p (tb d) -> p tb d", d=128), vsrc,
            [("VS", vq, t_) for t_ in range(KB)] + vbk, vbk, "vbuf%d" % kp)
        for h in (2 * kh, 2 * kh + 1):
            for kb in range(KB):
                ctx = st_step(kh, h, kb)
                if pend:
                    fn = pend.pop(0)
                    fn()
                if kb < KB - 1:
                    pend.append(lambda kh=kh, h=h, kb=kb, c=ctx: fin_step(kh, h, kb, *c))
                else:
                    def last(kh=kh, h=h, kb=kb, c=ctx):
                        fin_step(kh, h, kb, *c)
                        head_tail(h)
                    pend.append(last)
    for fn in pend:
        fn()
    if "ot" in tapd:
        DMA("gp", tapd["ot"], OT, K("big", 0, 16384), [], "tapot")

    for sH in range(2):
        for c in range(NCH):
            MM(p2b[0:RANK, sH * 512:(sH + 1) * 512], laabO[:, c * RANK:(c + 1) * RANK],
               OT[:, c * 1024 + sH * 512: c * 1024 + (sH + 1) * 512],
               c == 0, c == NCH - 1,
               K("la4b") + K("big", c * 1024 + sH * 512, c * 1024 + (sH + 1) * 512),
               pkh("p2b", sH * 512, (sH + 1) * 512))
    sch.add("dv", lambda e: e.tensor_scalar_mul(out=sqa[:], in0=p2b[0:RANK, :], scalar1=2.0),
            reads=PKEY["p2b"], writes=K("sqa"))
    DMA("gp", wpanb[:, 0, :], panel_ap("wo", 0), K("wpanb", 0, 2048), K("wpanb", 0, 2048), "pano0")
    DMA("gp", wpanb[:, 1, :], panel_ap("wo", 1), K("wpanb", 2048, 4096), K("wpanb", 2048, 4096), "pano1")
    for ob in range(NCH):
        par = ob % 2
        wb = wpanb[:, par, :]
        wbk = K("wpanb", par * 2048, (par + 1) * 2048)
        pw, pwn = (p2a, "p2a") if par == 0 else (p2b, "p2b")
        for sH in range(2):
            sl = slice(sH * 512, (sH + 1) * 512)
            for c in range(NCH):
                MM(pw[:, sl], wb[:, c * 128:(c + 1) * 128],
                   OT[:, c * 1024 + sH * 512: c * 1024 + (sH + 1) * 512], c == 0, False,
                   wbk + K("big", c * 1024 + sH * 512, c * 1024 + (sH + 1) * 512),
                   pkh(pwn, sH * 512, (sH + 1) * 512))
            MM(pw[:, sl], lorab[0:RANK, ob * 128:(ob + 1) * 128], sqa[0:RANK, sl],
               False, True, K("lorab") + K("sqa"), pkh(pwn, sH * 512, (sH + 1) * 512))
        if ob + 2 < NCH:
            DMA("gp", wpanb[:, par, :], panel_ap("wo", ob + 2), wbk, wbk, "pano%d" % par)
        sch.add("dv", lambda e, o=s32[:, ob * 1024:(ob + 1) * 1024], p=pw:
                e.tensor_tensor(out=o, in0=p[:, 0:1024], in1=o, op=ALU.add),
                reads=PKEY[pwn] + K("s32", ob * 1024, (ob + 1) * 1024),
                writes=K("s32", ob * 1024, (ob + 1) * 1024))
    if "r1" in tapd:
        DMA("gp", tapd["r1"], s32[:], K("s32"), [], "tapr1")

    def layernorm(emit_bf16):
        for c in range(NCH):
            csl = slice(c * 1024, (c + 1) * 1024)
            ck = K("s16", c * 1024, (c + 1) * 1024)
            CAST("dv", s16[:, csl], s32[:, csl], K("s32", c * 1024, (c + 1) * 1024), ck)
            sch.add("ac", lambda e, i=s16[:, csl], o=big[:, csl]:
                    e.activation(out=o, in_=i, func=AF.Square),
                    reads=ck, writes=K("big", c * 1024, (c + 1) * 1024))
            for sH in range(2):
                MM(prow[0:1, sH * 512:(sH + 1) * 512], onesb[:],
                   s16[:, c * 1024 + sH * 512: c * 1024 + (sH + 1) * 512],
                   c == 0, c == NCH - 1,
                   K("onesb") + ck, [PKEY["prow"][sH]])
                MM(p2c[0:1, sH * 512:(sH + 1) * 512], onesb[:],
                   big[:, c * 1024 + sH * 512: c * 1024 + (sH + 1) * 512],
                   c == 0, c == NCH - 1,
                   K("onesb") + K("big", c * 1024 + sH * 512, c * 1024 + (sH + 1) * 512),
                   [PKEY["p2c"][sH]])
        sch.add("dv", lambda e: e.tensor_scalar_mul(out=rowsA[:], in0=prow[0:1, :], scalar1=1.0 / H),
                reads=PKEY["prow"], writes=K("rowsA"))
        sch.add("dv", lambda e: e.tensor_scalar_mul(out=rowsB[:], in0=p2c[0:1, :], scalar1=1.0 / H),
                reads=PKEY["p2c"], writes=K("rowsB"))
        TT("dv", ALU.mult, rowsC[:], rowsA[:], rowsA[:], K("rowsA"), K("rowsC"))
        TT("dv", ALU.subtract, rowsB[:], rowsB[:], rowsC[:], K("rowsB") + K("rowsC"), K("rowsB"))
        sch.add("ac", lambda e: e.activation(out=rowsB[:], in_=rowsB[:], func=AF.Sqrt,
                                             bias=cst[0:1, 0:1]),
                reads=K("rowsB") + K("cst"), writes=K("rowsB"))
        sch.add("ac", lambda e: e.activation(out=rowsC[:], in_=rowsB[:], func=AF.Ln),
                reads=K("rowsB"), writes=K("rowsC"))
        sch.add("ac", lambda e: e.activation(out=rowsC[:], in_=rowsC[:], func=AF.Exp, scale=-1.0),
                reads=K("rowsC"), writes=K("rowsC"))
        for s0 in (0, 512):
            MM(p2a[:, s0:s0 + 512], ones32[:], rowsA[0:1, s0:s0 + 512], True, True,
               K("ones32") + K("rowsA"), pkh("p2a", s0, s0 + 512))
            MM(p2b[:, s0:s0 + 512], ones32[:], rowsC[0:1, s0:s0 + 512], True, True,
               K("ones32") + K("rowsC"), pkh("p2b", s0, s0 + 512))
        sch.add("dv", lambda e: e.tensor_copy(out=mubc[:], in_=p2a[:, 0:1024]),
                reads=PKEY["p2a"], writes=K("mubc"))
        sch.add("ac", lambda e: e.copy(out=rstdbc[:], in_=p2b[:, 0:1024]),
                reads=PKEY["p2b"], writes=K("rstdbc"))
        for c in range(NCH):
            csl = slice(c * 1024, (c + 1) * 1024)
            sk = K("s32", c * 1024, (c + 1) * 1024)
            TT("dv", ALU.subtract, s32[:, csl], s32[:, csl], mubc[:], sk + K("mubc"), sk)
            TT("dv", ALU.mult, s32[:, csl], s32[:, csl], rstdbc[:], sk + K("rstdbc"), sk)
            if emit_bf16:
                CAST("dv", s16[:, csl], s32[:, csl], sk, K("s16", c * 1024, (c + 1) * 1024))

    layernorm(True)
    if "xln" in tapd:
        DMA("gp", tapd["xln"], s32[:], K("s32"), [], "tapxln")

    fuse = big[:, 0:32768]
    for s2 in range(2):
        toff = s2 * 512
        for ib in range(NIB):
            par = ib % 2
            gb = gpanb[:, par, :]
            ub = upanb[:, par, :]
            gbk = K("gpanb", par * 2048, (par + 1) * 2048)
            ubk = K("upanb", par * 2048, (par + 1) * 2048)
            gib = s2 * NIB + ib
            if gib < 2:
                DMA("gp", gb.rearrange("p (c q) -> p c q", q=128),
                    I["wg"][ib:ib + 1].rearrange("a p c q -> p (a c) q"), gbk, gbk, "gp8%d" % par)
                DMA("gp", ub.rearrange("p (c q) -> p c q", q=128),
                    I["wu"][ib:ib + 1].rearrange("a p c q -> p (a c) q"), ubk, ubk, "up8%d" % par)
            pg = p2a[:, par * 512:(par + 1) * 512]
            pgk = pkh("p2a", par * 512, (par + 1) * 512)
            pu2 = p2b[:, par * 512:(par + 1) * 512]
            puk = pkh("p2b", par * 512, (par + 1) * 512)
            for c in range(NCH):
                MM(pg, gb[:, c * 128:(c + 1) * 128],
                   s16[:, c * 1024 + toff: c * 1024 + toff + 512], c == 0, c == NCH - 1,
                   gbk + K("s16", c * 1024 + toff, c * 1024 + toff + 512), pgk)
            for c in range(NCH):
                MM(pu2, ub[:, c * 128:(c + 1) * 128],
                   s16[:, c * 1024 + toff: c * 1024 + toff + 512], c == 0, c == NCH - 1,
                   ubk + K("s16", c * 1024 + toff, c * 1024 + toff + 512), puk)
            nx = gib + 2
            if nx < 2 * NIB:
                nib_, npar = nx % NIB, nx % 2
                ngb = gpanb[:, npar, :]
                nub = upanb[:, npar, :]
                ngbk = K("gpanb", npar * 2048, (npar + 1) * 2048)
                nubk = K("upanb", npar * 2048, (npar + 1) * 2048)
                DMA("gp", ngb.rearrange("p (c q) -> p c q", q=128),
                    I["wg"][nib_:nib_ + 1].rearrange("a p c q -> p (a c) q"), ngbk, ngbk, "gp8%d" % npar)
                DMA("gp", nub.rearrange("p (c q) -> p c q", q=128),
                    I["wu"][nib_:nib_ + 1].rearrange("a p c q -> p (a c) q"), nubk, nubk, "up8%d" % npar)
            gv = gbuf[:, par, :]
            gvk = K("gbuf", par * 512, (par + 1) * 512)
            sch.add("ac", lambda e, o=gv, i=pg, sc=sgc[:, ib:ib + 1]:
                    e.activation(out=o, in_=i, func=AF.Silu, scale=sc),
                    reads=pgk + K("sgc"), writes=gvk)
            sch.add("dv", lambda e, o=fuse[:, ib * 512:(ib + 1) * 512], i=pu2, sc=suc[:, ib:ib + 1], gg=gv:
                    e.scalar_tensor_tensor(out=o, in0=i, scalar=sc, in1=gg, op0=ALU.mult, op1=ALU.mult),
                    reads=puk + K("suc") + gvk, writes=K("big", ib * 512, (ib + 1) * 512))
        for ob in range(NCH):
            pd = p2c[:, (ob % 2) * 512:((ob % 2) + 1) * 512]
            pdk = pkh("p2c", (ob % 2) * 512, ((ob % 2) + 1) * 512)
            for half in range(2):
                hp = (ob * 2 + half + s2) % 2
                dv_ = wdb[:, hp, :]
                dvk = K("wdb", hp * 4096, (hp + 1) * 4096)
                DMA("gp", dv_.rearrange("p (c q) -> p c q", q=128),
                    I["wd"][ob:ob + 1].rearrange("a p c q -> p (a c) q")[:, half * 32:(half + 1) * 32, :],
                    dvk, dvk, "wd%d" % hp)
                for u in range(32):
                    c = half * 32 + u
                    MM(pd, dv_[:, u * 128:(u + 1) * 128], fuse[:, c * 512:(c + 1) * 512],
                       c == 0, c == NIB - 1,
                       dvk + K("big", c * 512, (c + 1) * 512), pdk)
            sch.add("dv", lambda e, o=s32[:, ob * 1024 + toff: ob * 1024 + toff + 512],
                    i=pd, sc=sdc[:, ob:ob + 1]:
                    e.scalar_tensor_tensor(out=o, in0=i, scalar=sc, in1=o, op0=ALU.mult, op1=ALU.add),
                    reads=pdk + K("sdc") + K("s32", ob * 1024 + toff, ob * 1024 + toff + 512),
                    writes=K("s32", ob * 1024 + toff, ob * 1024 + toff + 512))
    if "r2" in tapd:
        DMA("gp", tapd["r2"], s32[:], K("s32"), [], "tapr2")

    layernorm(False)
    out_dma = DMA("gp", yout.rearrange("c p t -> p c t"),
                  s32[:].rearrange("p (c t) -> p c t", t=T), K("s32"), [], "yout")
    sch.add("gp", lambda e: e.engine_nop(), deps=(out_dma,))

    cnt = sch.finalize()
    nc._sched = sch
    with ExitStack() as stack:
        E = stack.enter_context
        sems = {}
        for key in sorted(cnt):
            sems[key] = E(nc.semaphore("sem_" + key.replace(":", "_")))
        block = E(nc.Block())

        @block.tensor
        def _(eng):
            sch.emit("pe", eng, sems)

        @block.vector
        def _(eng):
            sch.emit("dv", eng, sems)

        @block.scalar
        def _(eng):
            sch.emit("ac", eng, sems)

        @block.sync
        def _(eng):
            sch.emit("sy", eng, sems)

        @block.gpsimd
        def _(eng):
            sch.emit("gp", eng, sems)

    return nc


def kernel(**inputs):
    in_maps, own_all = host_prep(inputs)
    nc = build_program()
    res = run_bass_kernel_spmd(nc, in_maps, core_ids=list(range(8)))
    y = np.zeros((B, S, H), dtype=np.float32)
    for c in range(8):
        b, own = own_all[c]
        yc = np.asarray(res.results[c]["yout"])
        y[b, own] = yc.reshape(H, T).T
    return y


def run_dev(inputs, taps=()):
    in_maps, own_all = host_prep(inputs)
    nc = build_program(taps=taps)
    res = run_bass_kernel_spmd(nc, in_maps, core_ids=list(range(8)))
    return res, own_all
